# version 03
# baseline (speedup 1.0000x reference)
import os as _os
import numpy as np
import ml_dtypes

import concourse.bass as bass
import concourse.mybir as mybir
from concourse.bass_utils import run_bass_kernel_spmd

F32 = mybir.dt.float32
F32R = mybir.dt.float32r
BF16 = mybir.dt.bfloat16
AF = mybir.ActivationFunctionType
ALU = mybir.AluOpType
AX = mybir.AxisListType

NCORE = 8
NSEQ = 2
T = 2048
D = 1024
DC = D // 128
NH = 16
HD = 64
HP = NH // 2
LAT = 256
RD = 32
I_FF = 2048
IT = I_FF // 128
NE = 8
EPS = 1e-5
NEG = -1.0e30


class Ctr:
    def __init__(self, nc, name):
        self.h = nc.alloc_semaphore(name)
        self.v = 0

    def inc(self, ins, k=1):
        ins.then_inc(self.h, k)
        self.v += k
        return ins

    def dinc(self, ins):
        return self.inc(ins, 16)

    def wait(self, eng, v=None):
        eng.wait_ge(self.h, self.v if v is None else v)


def build_host_inputs(inputs):
    f = np.float32
    A = lambda a: np.ascontiguousarray(np.asarray(a, f))
    g = inputs
    xt = A(np.asarray(g["L0_attn_x"], f).transpose(0, 2, 1))
    wqkv = A(g["L0_attn_kernel"])
    kvdT = A(np.asarray(g["L0_ffn_kv_d"], f).T)
    qdT = A(np.asarray(g["L0_ffn_q_d"], f).T)
    kuT = A(np.asarray(g["L0_ffn_k_u"], f).T)
    quT = A(np.asarray(g["L0_ffn_q_u"], f).T)
    vuT = A(np.asarray(g["L0_ffn_v_u"], f).T)
    rqT = A(np.asarray(g["L0_ffn_rope_q_w"], f).T)
    rkT = A(np.asarray(g["L0_ffn_rope_k_w"], f).T)
    perm = np.arange(NH * RD)
    i = perm % RD
    perm = (perm - i) + ((i + 16) % RD)
    rqTp = A(rqT[:, perm])
    rkTp = A(rkT[:, perm])
    owT = A(np.asarray(g["L0_ffn_o_w"], f).T)
    rwT8 = np.zeros((D, 8), f)
    rwT8[:, :7] = np.asarray(g["L0_ffn_router_w"], f).T
    rbias8 = np.zeros((1, 8), f)
    rbias8[0, :7] = np.asarray(g["L0_ffn_routing_bias"], f)
    ln1w = A(np.asarray(g["L0_ffn_ln1_w"], f).reshape(1, D))
    ln2w = A(np.asarray(g["L0_ffn_ln2_w"], f).reshape(1, D))

    gates = np.concatenate([np.asarray(g["L0_ffn_shared_gate"], f),
                            np.asarray(g["L0_ffn_routed_gate"], f)], 0)
    ups = np.concatenate([np.asarray(g["L0_ffn_shared_up"], f),
                          np.asarray(g["L0_ffn_routed_up"], f)], 0)
    downs = np.concatenate([np.asarray(g["L0_ffn_shared_down"], f),
                            np.asarray(g["L0_ffn_routed_down"], f)], 0)
    guT = np.empty((NE, D, 2 * I_FF), f)
    g4 = guT.reshape(NE, D, IT, 256)
    g4[:, :, :, 0:128] = gates.transpose(0, 2, 1).reshape(NE, D, IT, 128)
    g4[:, :, :, 128:256] = ups.transpose(0, 2, 1).reshape(NE, D, IT, 128)
    dnT = A(downs.transpose(0, 2, 1))

    invf = 1.0 / (10000.0 ** (np.arange(0, RD, 2, dtype=np.float64) / RD))
    fr = np.arange(T, dtype=np.float64)[None, :] * invf[:, None]
    cos32 = np.cos(np.concatenate([fr, fr], 0))
    sin32 = np.sin(np.concatenate([fr, fr], 0))
    sinx32 = sin32.copy()
    sinx32[:16] *= -1.0
    cosT = A(np.tile(cos32, (4, 1)))
    sinxT = A(np.tile(sinx32, (4, 1)))
    cosqT = A(cosT * 0.125)
    sinxqT = A(sinxT * 0.125)
    kk = np.arange(128)[:, None]
    qq = np.arange(512)[None, :]
    masks = np.ascontiguousarray(
        np.stack([(kk + j * 128 <= qq) for j in range(4)]).astype(ml_dtypes.bfloat16))
    ones1 = np.ones((1, 128), f)

    common = dict(wqkv=wqkv, kvdT=kvdT, qdT=qdT, kuT=kuT, quT=quT, vuT=vuT,
                  rqT=rqT, rqTp=rqTp, rkT=rkT, rkTp=rkTp, owT=owT,
                  rwT8=rwT8, rbias8=rbias8, ln1w=ln1w, ln2w=ln2w,
                  guT=np.ascontiguousarray(guT), dnT=dnT, cosT=cosT, sinxT=sinxT,
                  cosqT=cosqT, sinxqT=sinxqT,
                  masks=masks, ones1=ones1)
    in_maps = []
    for c in range(NCORE):
        m = dict(common)
        m["xt"] = np.ascontiguousarray(xt[c * NSEQ:(c + 1) * NSEQ])
        in_maps.append(m)
    return in_maps


class MK:
    def __init__(self, nc, debug=False, n_seq=NSEQ, last_phase="E", stop_at=None,
                 skip_prep=False):
        self.nc = nc
        self.debug = debug
        self.n_seq = n_seq
        self.last_phase = last_phase
        self.stop_at = stop_at
        self.skip_prep = skip_prep
        self.Te, self.V, self.S, self.P, self.Y = nc.tensor, nc.vector, nc.scalar, nc.gpsimd, nc.sync
        self._declare_hbm()
        self._alloc_sbuf()
        self._alloc_psum()
        self._alloc_sems()
        self.out_dma_waits = []

    def _declare_hbm(self):
        nc = self.nc
        d = nc.dram_tensor
        self.xt = d("xt", [NSEQ, D, T], F32, kind="ExternalInput").ap()
        self.wqkv = d("wqkv", [3, D, D], F32, kind="ExternalInput").ap()
        self.kvdT = d("kvdT", [D, LAT], F32, kind="ExternalInput").ap()
        self.qdT = d("qdT", [D, LAT], F32, kind="ExternalInput").ap()
        self.kuT = d("kuT", [LAT, D], F32, kind="ExternalInput").ap()
        self.quT = d("quT", [LAT, D], F32, kind="ExternalInput").ap()
        self.vuT = d("vuT", [LAT, D], F32, kind="ExternalInput").ap()
        self.rqT = d("rqT", [LAT, NH * RD], F32, kind="ExternalInput").ap()
        self.rqTp = d("rqTp", [LAT, NH * RD], F32, kind="ExternalInput").ap()
        self.rkT = d("rkT", [D, NH * RD], F32, kind="ExternalInput").ap()
        self.rkTp = d("rkTp", [D, NH * RD], F32, kind="ExternalInput").ap()
        self.owT = d("owT", [D, D], F32, kind="ExternalInput").ap()
        self.rwT8 = d("rwT8", [D, 8], F32, kind="ExternalInput").ap()
        self.rbias8 = d("rbias8", [1, 8], F32, kind="ExternalInput").ap()
        self.ln1w = d("ln1w", [1, D], F32, kind="ExternalInput").ap()
        self.ln2w = d("ln2w", [1, D], F32, kind="ExternalInput").ap()
        self.guT = d("guT", [NE, D, 2 * I_FF], F32, kind="ExternalInput").ap()
        self.dnT = d("dnT", [NE, I_FF, D], F32, kind="ExternalInput").ap()
        self.cosT = d("cosT", [128, T], F32, kind="ExternalInput").ap()
        self.sinxT = d("sinxT", [128, T], F32, kind="ExternalInput").ap()
        self.cosqT = d("cosqT", [128, T], F32, kind="ExternalInput").ap()
        self.sinxqT = d("sinxqT", [128, T], F32, kind="ExternalInput").ap()
        self.masks = d("masks", [4, 128, 512], BF16, kind="ExternalInput").ap()
        self.ones1 = d("ones1", [1, 128], F32, kind="ExternalInput").ap()

        self.out_t = d("out_t", [NSEQ, D, T], F32, kind="ExternalOutput").ap()
        scr = "ExternalOutput" if self.debug else "Internal"
        self.gu_bf = d("gu_bf", [NE, D, 2 * I_FF], BF16, kind="Internal").ap()
        self.dn_bf = d("dn_bf", [NE, I_FF, D], BF16, kind="Internal").ap()
        self.u_scr = d("u_scr", [D, T], F32, kind=scr).ap()
        self.out0_scr = d("out0_scr", [D, T], F32, kind=scr).ap()
        self.h1_scr = d("h1_scr", [D, T], F32, kind=scr).ap()
        self.s_scr = d("s_scr", [D, T], F32, kind=scr).ap()
        self.rows_scr = d("rows_scr", [8, T], F32, kind=scr).ap()
        self.wgt_scr = d("wgt_scr", [8, T], F32, kind=scr).ap()
        if self.debug:
            self.dbg_ain = d("dbg_ain", [D, T], F32, kind="ExternalOutput").ap()
            self.dbg_min = d("dbg_min", [D, T], F32, kind="ExternalOutput").ap()
            self.dbg_y2 = d("dbg_y2", [D, T], F32, kind="ExternalOutput").ap()
            self.dbg_wgt = d("dbg_wgt", [8, T], F32, kind="ExternalOutput").ap()
            self.dbg_rowt = d("dbg_rowt", [128, 96], F32, kind="ExternalOutput").ap()

    def _alloc_sbuf(self):
        a = self.nc.alloc_sbuf_tensor
        self.AR_A = a("AR_A", [128, 8 * T], BF16)
        self.AR_B = a("AR_B", [128, 8 * T], BF16)
        self.AR_C = a("AR_C", [128, 8 * T], BF16)
        self.V65 = a("V65", [128, 16 * NH * 65], BF16)
        self.QKR = a("QKR", [128, 4 * T], BF16)
        self.LATB = a("LATB", [128, 4 * T], BF16)
        self.ROT = a("ROT", [128, 2 * T], F32)
        self.WRG = a("WRG", [128, 4 * D], BF16)
        self.WRGB = a("WRGB", [128, 1024], BF16)
        self.MSK = a("MSK", [128, 4 * 512], BF16)
        self.ONESR = a("ONESR", [1, 128], F32R)
        self.ONESB = a("ONESB", [128, 1], BF16)
        self.ROWT = a("ROWT", [128, 10 * 16], F32)
        self.ROWB = a("ROWB", [128, 3592], F32R)
        self.RTR = a("RTR", [128, 224], F32)
        self.RW8 = a("RW8", [128, 8 * 8], BF16)
        self.WCOL = a("WCOL", [128, 16], F32)
        self.IDF = a("IDF", [128, 128], F32)
        rem = self.nc.sbuf_bytes_remaining
        assert rem >= 0, f"SBUF over budget by {-rem} bytes"

    def rt_mu(self):    return self.ROWT.ap()[:, 0:16]
    def rt_msq(self):   return self.ROWT.ap()[:, 16:32]
    def rt_var(self):   return self.ROWT.ap()[:, 32:48]
    def rt_rstd(self):  return self.ROWT.ap()[:, 48:64]
    def rt_r2(self):    return self.ROWT.ap()[:, 64:80]
    def rt_tmp(self):   return self.ROWT.ap()[:, 80:96]
    def rt_n1(self):    return self.ROWT.ap()[:, 96:112]
    def rt_n2(self):    return self.ROWT.ap()[:, 112:128]
    def rs_mu(self):    return self.ROT.ap()[0:1, 0:T]
    def rs_msq(self):   return self.ROT.ap()[0:1, T:2 * T]
    def ROWR(self):     return self.ROWB.ap()[0:1, 0:T]
    def WROW(self):     return self.ROWB.ap()[0:1, T:T + D]
    def RRrow(self):    return self.ROWB.ap()[0:1, T + D:T + D + 512]
    def BIAS8r(self):   return self.ROWB.ap()[0:1, T + D + 512:T + D + 520]
    def URt(self):      return self.QKR.ap()[:, 0:T]
    def SQRt(self):     return self.QKR.ap()[:, T:2 * T]
    def WGTrows(self):  return self.LATB.ap().bitcast(F32)[0:8, 0:T]

    def _alloc_psum(self):
        p = self.nc.alloc_psum_tensor
        self.PS0 = p("PS0", [128, 1024], F32)
        self.PS1 = p("PS1", [128, 1024], F32)
        self.PS2 = p("PS2", [128, 512], F32)
        self.PS3 = p("PS3", [128, 512], F32)
        self.PS4 = p("PS4", [128, 512], F32)
        self.PS5 = p("PS5", [128, 512], F32)

    def _alloc_sems(self):
        for n in ["c_pld", "c_pst", "c_xt", "c_wld", "c_vev", "c_qk", "c_qkev",
                  "c_sc", "c_exp", "c_msk", "c_av", "c_yev", "c_rcp", "c_rb",
                  "c_zn", "c_xld", "c_u", "c_udma",
                  "c_uld", "c_sq", "c_stat", "c_rows", "c_rbc", "c_nrm",
                  "c_o0", "c_ain", "c_lat", "c_rope", "c_scat",
                  "c_oproj", "c_h1", "c_sdma", "c_hld",
                  "c_rl", "c_rv", "c_rw", "c_rt2", "c_rtr",
                  "c_gld", "c_gmm", "c_gat", "c_silu", "c_act", "c_dld", "c_eo", "c_cmb",
                  "c_sld", "c_wbc", "c_out", "c_misc", "c_vx", "c_pl", "c_ev2",
                  "c_cst", "c_yld", "c_rowd", "c_rbm",
                  "c_ua", "c_ub", "c_o0a", "c_o0b", "c_sx"]:
            setattr(self, n, Ctr(self.nc, n))

    def rsqrt_rows(self, out_ap, v_ap):
        V, S = self.V, self.S
        n1, n2 = self.rt_n1(), self.rt_n2()
        self.c_vx.wait(S)
        self.ssync(S.activation(n1, v_ap, AF.Ln))
        self.ssync(S.activation(out_ap, n1, AF.Exp, scale=-0.5))
        self.c_sx.wait(V)
        for _ in range(2):
            self.vsync(V.tensor_tensor(n1, out_ap, out_ap, ALU.mult))
            self.vsync(V.tensor_tensor(n2, v_ap, n1, ALU.mult))
            self.vsync(V.tensor_scalar(n1, n2, -0.5, 1.5, ALU.mult, ALU.add))
            yy = self.vsync(V.tensor_tensor(out_ap, out_ap, n1, ALU.mult))
        return yy

    def ssync(self, ins):
        self.c_sx.inc(ins)
        self.c_sx.wait(self.S)
        return ins

    def vsync(self, ins):
        self.c_vx.inc(ins)
        self.c_vx.wait(self.V)
        return ins

    def prep_and_consts(self):
        P, Y = self.P, self.Y
        self.c_misc.dinc(Y.dma_start(
            out=self.MSK.ap()[:, :].rearrange("p (j n) -> p j n", j=4),
            in_=self.masks[:, :, :].rearrange("j p n -> p j n")))
        self._msk_ld = self.c_misc.v
        self.c_cst.dinc(P.dma_start(out=self.ONESR.ap()[:, :], in_=self.ones1[:, :]))
        self.c_cst.dinc(P.dma_start(
            out=self.ONESB.ap()[:, :], in_=self.ones1[:, :].rearrange("a b -> b a")))
        self.c_cst.dinc(P.dma_start(out=self.BIAS8r(), in_=self.rbias8[:, :]))
        self.c_cst.dinc(P.dma_start(
            out=self.RW8.ap()[:, :].rearrange("p (c e) -> p c e", e=8),
            in_=self.rwT8[:, :].rearrange("(c p) e -> p c e", p=128)))
        self.c_cst.dinc(P.dma_start(
            out=self.WCOL.ap()[:, 0:8],
            in_=self.ln1w[:, :].rearrange("a (c p) -> (a p) c", p=128),
            allow_slow_non_contiguous=True))
        self.c_cst.dinc(P.dma_start(
            out=self.WCOL.ap()[:, 8:16],
            in_=self.ln2w[:, :].rearrange("a (c p) -> (a p) c", p=128),
            allow_slow_non_contiguous=True))
        idn = self.nc.inline_tensor(np.eye(128, dtype=np.float32), "ident128")
        self.c_misc.dinc(Y.dma_start(out=self.IDF.ap()[:, :], in_=idn.ap()[:, :]))

        if self.skip_prep:
            self.prep_sts = 0
            self._prep_jobs = []
            return
        jobs = []
        for e in range(NE):
            for r in range(DC):
                jobs.append(("gu", e, r))
        for e in range(NE):
            for r in range(0, IT, 4):
                jobs.append(("dn", e, r))
        self._prep_jobs = jobs
        self._prep_n = 0
        self._prep_store_done = {}
        self.prep_sts = len(jobs) * 16

    def _emit_prep(self, k):
        P, Y = self.P, self.Y
        stage = self.AR_B.ap()
        while k > 0 and self._prep_jobs:
            kind, e, r = self._prep_jobs.pop(0)
            n = self._prep_n
            slot = stage[:, (n % 2) * 8192:(n % 2) * 8192 + 4096]
            if n >= 2:
                P.wait_ge(self.c_pst.h, self._prep_store_done[n - 2])
            if kind == "gu":
                self.c_pld.dinc(P.dma_start(out=slot, in_=self.guT[e, r * 128:(r + 1) * 128, :]))
                Y.wait_ge(self.c_pld.h, self.c_pld.v)
                self.c_pst.dinc(Y.dma_start(out=self.gu_bf[e, r * 128:(r + 1) * 128, :], in_=slot))
            else:
                src = self.dnT[e, r * 128:(r + 4) * 128, :].rearrange("(a p) n -> p a n", p=128)
                dst = self.dn_bf[e, r * 128:(r + 4) * 128, :].rearrange("(a p) n -> p a n", p=128)
                slot3 = slot.rearrange("p (a n) -> p a n", a=4)
                self.c_pld.dinc(P.dma_start(out=slot3, in_=src))
                Y.wait_ge(self.c_pld.h, self.c_pld.v)
                self.c_pst.dinc(Y.dma_start(out=dst, in_=slot3))
            self._prep_store_done[n] = self.c_pst.v
            self._prep_n = n + 1
            k -= 1

    def phase_A(self, s):
        nc, Te, V, S, P, Y = self.nc, self.Te, self.V, self.S, self.P, self.Y
        XT = self.AR_A.ap()
        WV = self.LATB.ap()
        V65 = self.V65.ap().rearrange("p (k h c) -> p k h c", k=16, h=NH)
        QKR = self.QKR.ap()
        ZST = self.AR_C.ap().bitcast(F32)
        PR = self.AR_C.ap()
        PR_OFF = 8192
        ROT = self.ROT.ap()
        WRG = self.WRG.ap()

        for dc in range(DC):
            self.c_xt.dinc(P.dma_start(out=XT[:, dc * T:(dc + 1) * T],
                                       in_=self.xt[s, dc * 128:(dc + 1) * 128, :]))
        for dc in range(DC):
            self.c_wld.dinc(P.dma_start(
                out=WV[:, dc * D:(dc + 1) * D],
                in_=self.wqkv[2, dc * 128:(dc + 1) * 128, :]))
        if self.stop_at == "xt":
            return
        self.c_xt.wait(V)
        ms = V.memset(self.V65.ap()[:], 1.0)
        self.c_vev.inc(ms)

        if self.stop_at == "load":
            return
        xt_ready = self.c_xt.v
        wv_ready = self.c_wld.v
        Te.wait_ge(self.c_xt.h, xt_ready)
        Te.wait_ge(self.c_wld.h, wv_ready)
        vev_done = self.c_vev.v
        for tt in range(16):
            ps = [self.PS0, self.PS1][tt % 2]
            if tt >= 2:
                self.c_vev.wait(Te, vev_done + (tt - 2) + 1)
            for eb in range(2):
                for dc in range(DC):
                    mm = Te.matmul(ps[:, eb * 512:(eb + 1) * 512],
                                   XT[:, dc * T + tt * 128: dc * T + tt * 128 + 128],
                                   WV[:, dc * D + eb * 512: dc * D + eb * 512 + 512],
                                   start=(dc == 0), stop=(dc == DC - 1))
            self.c_sc.inc(mm)
            self.c_sc.wait(V)
            ev = V.tensor_copy(V65[:, tt, :, 0:64], ps[:].rearrange("p (h c) -> p h c", c=64))
            self.c_vev.inc(ev)

        if self.stop_at == "vprod":
            return
        def load_w_slices(p, slot):
            ten = self._wrg_tenant[slot]
            if ten is not None:
                P.wait_ge(self.c_qk.h, ten)
            for wi, w_h in ((0, self.wqkv), (1, self.wqkv)):
                src = (self.wqkv[0, :, p * 128:(p + 1) * 128] if wi == 0 else
                       self.wqkv[1, :, p * 128:(p + 1) * 128])
                dst = WRG[:, (2 * slot + wi) * D:(2 * slot + wi) * D + D]
                self.c_wld.dinc(P.dma_start(
                    out=dst.rearrange("p (c m) -> p c m", m=128),
                    in_=src.rearrange("(c p) m -> p c m", p=128)))

        def qk_prod_group(p, slot, wi, tb):
            ring = (2 * p + wi) % 4
            wsl = WRG[:, (2 * slot + wi) * D:(2 * slot + wi) * D + D]
            self.c_qkev.wait(Te)
            for dc in range(DC):
                mm = Te.matmul(self.PS5[:, :],
                               wsl[:, dc * 128:(dc + 1) * 128],
                               XT[:, dc * T + tb * 512: dc * T + tb * 512 + 512],
                               start=(dc == 0), stop=(dc == DC - 1))
            self.c_qk.inc(mm)
            self.c_qk.wait(S)
            ev = S.activation(QKR[:, ring * T + tb * 512: ring * T + tb * 512 + 512],
                              self.PS5[:, :], AF.Copy, scale=(0.125 if wi == 0 else 1.0))
            self.c_qkev.inc(ev)
            return ev

        self.c_vev.wait(Te)
        load_w_slices(0, 0)
        wld_for_pair = {0: self.c_wld.v}
        self._emit_prep(12)
        Te.wait_ge(self.c_wld.h, wld_for_pair[0])
        for wi in range(2):
            for tb in range(4):
                qk_prod_group(0, 0, wi, tb)
        self._wrg_tenant[0] = self.c_qk.v

        for p in range(HP):
            if p + 1 < HP:
                load_w_slices(p + 1, (p + 1) % 2)
                wld_for_pair[p + 1] = self.c_wld.v
                self._wrg_pending = (p + 1) % 2
            rten = self._rot_tenant[p % 2]
            if rten is not None:
                Y.wait_ge(self.c_u.h, rten)
            xl = Y.dma_start(out=ROT[:, (p % 2) * T:(p % 2) * T + T],
                             in_=self.xt[s, p * 128:(p + 1) * 128, :])
            self.c_xld.dinc(xl)
            xld_here = self.c_xld.v
            self._emit_prep(12)

            self._attention_pair(
                p, s, stage_b=False,
                qk_interleave=(lambda qb, _p=p: [
                    qk_prod_group(_p + 1, (_p + 1) % 2, wi_tb // 4, wi_tb % 4)
                    for wi_tb in (2 * qb, 2 * qb + 1)] if _p + 1 < HP else None),
                qk_wait=(lambda eng, _p=p: eng.wait_ge(self.c_wld.h, wld_for_pair[_p + 1]))
                if p + 1 < HP else None,
                zdst=("zst", p))
            if p + 1 < HP:
                self._wrg_tenant[(p + 1) % 2] = self.c_qk.v

            V.wait_ge(self.c_xld.h, xld_here)
            ztile = ZST[:, (p % 2) * T:(p % 2) * T + T]
            ut = V.tensor_tensor(ztile, ztile, ROT[:, (p % 2) * T:(p % 2) * T + T], ALU.add)
            self.c_u.inc(ut)
            self._rot_tenant[p % 2] = self.c_u.v
            self.c_u.wait(Y)
            cu = [self.c_ua, self.c_ub][p % 2]
            cu.dinc(Y.dma_start(out=self.u_scr[p * 128:(p + 1) * 128, :], in_=ztile))
            self._zst_tenant[p % 2] = cu.v

        if self.stop_at == "attn":
            return
        self._ln_epilogue_A(s)

    def _attention_pair(self, p, s, stage_b, qk_interleave, qk_wait, zdst,
                        pre_wait=None):
        nc, Te, V, S, P = self.nc, self.Te, self.V, self.S, self.P
        QKR = self.QKR.ap()
        V65 = self.V65.ap().rearrange("p (k h c) -> p k h c", k=16, h=NH)
        PRbase = 8192 if not stage_b else 0
        PR = self.AR_C.ap() if not stage_b else self.WRG.ap()
        qring = (2 * p) % 4
        kring = (2 * p + 1) % 4
        qof = qring * T
        kof = kring * T

        if zdst[0] == "zst":
            ZST = self.AR_C.ap().bitcast(F32)
            zt = ZST[:, (p % 2) * T:(p % 2) * T + T]
            ten = self._zst_tenant[p % 2]
            if ten is not None:
                [self.c_ua, self.c_ub][p % 2].wait(V, ten)
        else:
            Y2 = self.AR_C.ap()
            zt = None

        if pre_wait is not None:
            pre_wait(Te)
        for qb in range(4):
            if qk_interleave is not None:
                if qk_wait is not None:
                    qk_wait(Te)
                qk_interleave(qb)
            nkt = 16 if not stage_b else (qb + 1) * 4
            y_started = False
            for kt in range(nkt):
                scps = [self.PS0, self.PS1][kt % 2]
                prev_exp = self._sc_tenant[kt % 2]
                if prev_exp is not None:
                    self.c_exp.wait(Te, prev_exp)
                slot = kt % 4
                prev = self._pr_tenant[slot]
                if prev is not None:
                    self.c_av.wait(S, prev)
                m1 = Te.matmul(scps[:, 0:512],
                               QKR[:, kof + kt * 128: kof + kt * 128 + 128][0:64, :],
                               QKR[0:64, qof + qb * 512: qof + qb * 512 + 512],
                               start=True, stop=True, tile_position=(0, 0))
                m2 = Te.matmul(scps[:, 512:1024],
                               QKR[:, kof + kt * 128: kof + kt * 128 + 128][64:128, :],
                               QKR[64:128, qof + qb * 512: qof + qb * 512 + 512],
                               start=True, stop=True, tile_position=(64, 0))
                self.c_sc.inc(m2)
                self.c_sc.wait(S)
                pr_slot = PR[:, PRbase + slot * 1024: PRbase + slot * 1024 + 1024] \
                    if not stage_b else PR[:, slot * 1024: slot * 1024 + 1024]
                ex = S.activation(pr_slot, scps[:, :], AF.Exp)
                self.c_exp.inc(ex)
                self._sc_tenant[kt % 2] = self.c_exp.v
                use_c = self.c_exp
                if stage_b and kt >= nkt - 4:
                    j = kt - (nkt - 4)
                    P.wait_ge(self.c_misc.h, self._msk_ld)
                    self.c_exp.wait(P)
                    mk1 = P.tensor_tensor(pr_slot[:, 0:512], pr_slot[:, 0:512],
                                          self.MSK.ap()[:, j * 512:(j + 1) * 512], ALU.mult)
                    mk2 = P.tensor_tensor(pr_slot[:, 512:1024], pr_slot[:, 512:1024],
                                          self.MSK.ap()[:, j * 512:(j + 1) * 512], ALU.mult)
                    self.c_msk.inc(mk2)
                    use_c = self.c_msk
                if kt == 0:
                    if self._ynorm_pending is not None:
                        self.c_yev.wait(Te, self._ynorm_pending[0])
                        self.c_rcp.wait(Te, self._ynorm_pending[1])
                use_c.wait(Te)
                a1 = Te.matmul(self.PS2[0:65, :], V65[:, kt, 2 * p, :],
                               pr_slot[:, 0:512], start=(kt == 0), stop=(kt == nkt - 1))
                a2 = Te.matmul(self.PS3[0:65, :], V65[:, kt, 2 * p + 1, :],
                               pr_slot[:, 512:1024], start=(kt == 0), stop=(kt == nkt - 1))
                self.c_av.inc(a2)
                self._pr_tenant[slot] = self.c_av.v
            for h in range(2):
                yps = [self.PS2, self.PS3][h]
                self.c_av.wait(V)
                self.c_rb.wait(V)
                with self.nc.allow_low_precision("softmax denominator reciprocal row as f32r for K=1 broadcast matmul"):
                    rc = V.reciprocal(self.RRrow(), yps[64:65, :])
                self.c_rcp.inc(rc)
                row0 = h * 64
                if zdst[0] == "zst":
                    dst = zt[:, qb * 512:(qb + 1) * 512][row0:row0 + 64, :]
                else:
                    Y2 = self.AR_C.ap()
                    dst = Y2[:, p * T + qb * 512: p * T + qb * 512 + 512][row0:row0 + 64, :]
                yc = V.tensor_copy(dst, yps[0:65, :][0:64, :])
                self.c_yev.inc(yc)
                self.c_rcp.wait(Te)
                self.c_zn.wait(Te)
                rb = Te.matmul(self.PS4[0:64, :], self.ONESR.ap()[:, 0:64], self.RRrow(),
                               start=True, stop=True)
                self.c_rb.inc(rb)
                self.c_rb.wait(V)
                zn = V.tensor_tensor(dst, dst, self.PS4[0:64, :], ALU.mult)
                self.c_zn.inc(zn)
            self._ynorm_pending = (self.c_yev.v, self.c_rcp.v)

    def _rows_from_stats(self):
        V, S, Y = self.V, self.S, self.Y
        mu, msq = self.rt_mu(), self.rt_msq()
        var, rstd, r2, tmp = self.rt_var(), self.rt_rstd(), self.rt_r2(), self.rt_tmp()
        self.c_rowd.wait(V)
        self.vsync(V.tensor_tensor(tmp, mu, mu, ALU.mult))
        self.vsync(V.tensor_tensor(var, msq, tmp, ALU.subtract))
        self.vsync(V.tensor_scalar_add(tmp, var, EPS))
        self.rsqrt_rows(rstd, tmp)
        self.vsync(V.scalar_tensor_tensor(r2, mu, -1.0, rstd, ALU.mult, ALU.mult))
        if self.debug:
            self.c_vx.wait(Y)
            self.c_rowd.dinc(Y.dma_start(out=self.dbg_rowt[:, :], in_=self.ROWT.ap()[:, 0:96]))
        self.c_vx.wait(Y)
        self.c_rowd.dinc(Y.dma_start(
            out=self.rows_scr[3:4, :].rearrange("g (p n) -> (g p) n", p=128),
            in_=self.ROWT.ap()[:, 48:64]))
        self.c_rowd.dinc(Y.dma_start(
            out=self.rows_scr[4:5, :].rearrange("g (p n) -> (g p) n", p=128),
            in_=self.ROWT.ap()[:, 64:80]))

    def _stats_pass(self, src_scr, nrm=1.0 / D):
        Te, V, S, P = self.Te, self.V, self.S, self.P
        self.c_sc.wait(P)
        for dc in range(DC):
            if dc > 0:
                self.c_stat.wait(P)
            self.c_uld.dinc(P.dma_start(out=self.URt(),
                                        in_=src_scr[dc * 128:(dc + 1) * 128, :]))
            self.c_uld.wait(S)
            if dc > 0:
                self.c_stat.wait(S)
            sq = S.activation(self.SQRt(), self.URt(), AF.Square)
            self.c_sq.inc(sq)
            self.c_uld.wait(Te)
            self.c_sq.wait(Te)
            mus = [self.PS2, self.PS3, self.PS4, self.PS5]
            for j in range(4):
                Te.matmul(mus[j][0:1, :], self.ONESB.ap()[:, :],
                          self.URt()[:, j * 512:(j + 1) * 512],
                          start=(dc == 0), stop=(dc == DC - 1))
            for j in range(4):
                ps = [self.PS0, self.PS1][j // 2]
                mm = Te.matmul(ps[0:1, (j % 2) * 512:(j % 2) * 512 + 512],
                               self.ONESB.ap()[:, :],
                               self.SQRt()[:, j * 512:(j + 1) * 512],
                               start=(dc == 0), stop=(dc == DC - 1))
            self.c_stat.inc(mm)
        self.c_stat.wait(V)
        Y = self.Y
        mu = self.rs_mu()
        msq = self.rs_msq()
        mus = [self.PS2, self.PS3, self.PS4, self.PS5]
        for j in range(4):
            V.tensor_scalar_mul(mu[:, j * 512:(j + 1) * 512], mus[j][0:1, :], float(nrm))
        for j in range(4):
            ps = [self.PS0, self.PS1][j // 2]
            mm = V.tensor_scalar_mul(msq[:, j * 512:(j + 1) * 512],
                                     ps[0:1, (j % 2) * 512:(j % 2) * 512 + 512], float(nrm))
        self.c_stat.inc(mm)
        self.c_stat.wait(Y)
        self.c_rowd.dinc(Y.dma_start(out=self.rows_scr[0:1, :], in_=mu))
        self.c_rowd.dinc(Y.dma_start(out=self.rows_scr[1:2, :], in_=msq))
        self.c_rowd.wait(Y)
        self.c_rowd.dinc(Y.dma_start(
            out=self.ROWT.ap()[:, 0:16],
            in_=self.rows_scr[0:1, :].rearrange("g (p n) -> (g p) n", p=128)))
        self.c_rowd.dinc(Y.dma_start(
            out=self.ROWT.ap()[:, 16:32],
            in_=self.rows_scr[1:2, :].rearrange("g (p n) -> (g p) n", p=128)))

    def _row_to_f32r(self, dst_r, src_ap, n):
        dd = self.P.dma_start(out=dst_r[:, 0:n], in_=src_ap[:, 0:n])
        self.c_rbc.dinc(dd)

    def _bcast_rows(self, rows_f32r, psums):
        Te = self.Te
        for (row, plist) in zip(rows_f32r, psums):
            for j, (ps, off, rof) in enumerate(plist):
                mm = Te.matmul(ps[:, off:off + 512], self.ONESR.ap()[:, :],
                               row[:, rof:rof + 512], start=True, stop=True)
        self.c_rbc.inc(mm)

    def _ln_epilogue_A(self, s):
        nc, Te, V, S, P, Y = self.nc, self.Te, self.V, self.S, self.P, self.Y
        self.c_ua.wait(P)
        self.c_ub.wait(P)
        for c in (self.c_exp, self.c_yev, self.c_rcp, self.c_zn, self.c_qkev):
            c.wait(Te)
        self._stats_pass(self.u_scr)
        self._rows_from_stats()
        self.c_rowd.wait(P)
        self._row_to_f32r(self.ROWR(), self.rows_scr[3:4, :], T)
        self.c_rbc.wait(Te)
        self.c_stat.wait(Te)
        for j in range(4):
            ps = [self.PS0, self.PS1][j // 2]
            mm = Te.matmul(ps[:, (j % 2) * 512:(j % 2) * 512 + 512], self.ONESR.ap()[:, :],
                           self.ROWR()[:, j * 512:(j + 1) * 512], start=True, stop=True)
        self.c_rbm.inc(mm)
        self.c_rbm.wait(P)
        self._row_to_f32r(self.ROWR(), self.rows_scr[4:5, :], T)
        self.c_rbc.wait(Te)
        for j in range(4):
            ps = [self.PS2, self.PS3, self.PS4, self.PS5][j]
            mm = Te.matmul(ps[:, :], self.ONESR.ap()[:, :],
                           self.ROWR()[:, j * 512:(j + 1) * 512], start=True, stop=True)
        self.c_rbm.inc(mm)
        ROT = self.ROT.ap()
        AIN = self.AR_B.ap()
        self.c_rbm.wait(V)
        self.c_u.wait(Y)
        self.c_rowd.wait(Y)
        for dc in range(DC):
            slot = ROT[:, (dc % 2) * T:(dc % 2) * T + T]
            if dc >= 2:
                [self.c_o0a, self.c_o0b][dc % 2].wait(Y)
            ld = Y.dma_start(out=slot, in_=self.u_scr[dc * 128:(dc + 1) * 128, :])
            self.c_yld.dinc(ld)
            self.c_yld.wait(V)
            for j in range(2):
                V.tensor_tensor(slot[:, j * 1024:(j + 1) * 1024],
                                slot[:, j * 1024:(j + 1) * 1024],
                                [self.PS0, self.PS1][j][:, :], ALU.mult)
            for j in range(4):
                ps = [self.PS2, self.PS3, self.PS4, self.PS5][j]
                o0 = V.tensor_tensor(slot[:, j * 512:(j + 1) * 512],
                                     slot[:, j * 512:(j + 1) * 512], ps[:, :], ALU.add)
            self.c_nrm.inc(o0)
            self.c_nrm.wait(Y)
            co = [self.c_o0a, self.c_o0b][dc % 2]
            co.dinc(Y.dma_start(out=self.out0_scr[dc * 128:(dc + 1) * 128, :], in_=slot))
        var, rstd, tmp = self.rt_var(), self.rt_rstd(), self.rt_tmp()
        t2 = self.rt_mu()
        self.c_nrm.wait(V)
        self.vsync(V.tensor_tensor(tmp, var, rstd, ALU.mult))
        self.vsync(V.tensor_tensor(t2, tmp, rstd, ALU.mult))
        self.vsync(V.tensor_scalar_add(tmp, t2, EPS))
        self.rsqrt_rows(t2, tmp)
        self.c_vx.wait(Y)
        st = Y.dma_start(out=self.rows_scr[5:6, :].rearrange("g (p n) -> (g p) n", p=128),
                         in_=t2)
        self.c_rowd.dinc(st)
        self.c_rowd.wait(P)
        self._row_to_f32r(self.ROWR(), self.rows_scr[5:6, :], T)
        self.c_rbc.dinc(P.dma_start(out=self.WROW(), in_=self.ln1w[:, :]))
        self.c_rbc.wait(Te)
        self.c_nrm.wait(Te)
        self.c_o0a.wait(Y)
        self.c_o0b.wait(Y)
        self.c_pst.wait(V)
        spans_even = [(self.PS0, 0), (self.PS0, 512), (self.PS1, 0), (self.PS1, 512)]
        spans_odd = [(self.PS2, 0), (self.PS3, 0), (self.PS4, 0), (self.PS5, 0)]
        for dc in range(DC):
            slot = ROT[:, (dc % 2) * T:(dc % 2) * T + T]
            if dc >= 2:
                self.c_ain.wait(Y, self._aindone[dc - 2])
            ld = Y.dma_start(out=slot, in_=self.out0_scr[dc * 128:(dc + 1) * 128, :])
            self.c_yld.dinc(ld)
            spans = [spans_even, spans_odd][dc % 2]
            if dc >= 2:
                self.c_ain.wait(Te, self._aindone[dc - 2])
            for j, (ps, off) in enumerate(spans):
                mm = Te.matmul(ps[:, off:off + 512],
                               self.WROW()[:, dc * 128:(dc + 1) * 128],
                               self.ROWR()[:, j * 512:(j + 1) * 512],
                               start=True, stop=True)
            self.c_rbm.inc(mm)
            self.c_yld.wait(V)
            self.c_rbm.wait(V)
            for j, (ps, off) in enumerate(spans):
                ai = V.tensor_tensor(AIN[:, dc * T + j * 512: dc * T + j * 512 + 512],
                                     slot[:, j * 512:(j + 1) * 512],
                                     ps[:, off:off + 512], ALU.mult)
            self.c_ain.inc(ai)
            self._aindone[dc] = self.c_ain.v
            AIN = self.AR_B.ap()
        if self.debug:
            self.c_ain.wait(P)
            for dc in range(DC):
                self.c_cst.dinc(P.dma_start(
                    out=self.dbg_ain[dc * 128:(dc + 1) * 128, :],
                    in_=self.AR_B.ap()[:, dc * T:(dc + 1) * T]))


    def _psum_quiesce(self):
        for c in (self.c_exp, self.c_yev, self.c_rcp, self.c_zn, self.c_qkev,
                  self.c_vev, self.c_stat, self.c_nrm, self.c_ain, self.c_rtr,
                  self.c_silu, self.c_act, self.c_cmb, self.c_wbc, self.c_lat,
                  self.c_rope, self.c_eo, self.c_oproj, self.c_rw):
            c.wait(self.Te)

    def phase_B(self, s):
        nc, Te, V, S, P, Y = self.nc, self.Te, self.V, self.S, self.P, self.Y
        AIN = self.AR_B.ap()
        LATB = self.LATB.ap()
        WRG = self.WRG.ap()
        WRGB = self.WRGB.ap()
        QKR = self.QKR.ap()
        RT = self.AR_A.ap()
        ROT = self.ROT.ap()
        V65 = self.V65.ap().rearrange("p (k h c) -> p k h c", k=16, h=NH)
        if s > 0:
            self.c_misc.dinc(Y.dma_start(
                out=self.MSK.ap()[:, :].rearrange("p (j n) -> p j n", j=4),
                in_=self.masks[:, :, :].rearrange("j p n -> p j n")))
            self._msk_ld = self.c_misc.v
        self._psum_quiesce()

        for wi, w_h in ((0, self.kvdT), (1, self.qdT)):
            self.c_wld.dinc(P.dma_start(
                out=WRG[:, wi * 2048:(wi + 1) * 2048].rearrange("p (c m) -> p c m", m=LAT),
                in_=w_h[:, :].rearrange("(c p) m -> p c m", p=128)))
        wld_b1 = self.c_wld.v
        Te.wait_ge(self.c_wld.h, wld_b1)
        self.c_ain.wait(Te)
        for wi in range(2):
            for li in range(2):
                for tb in range(4):
                    self.c_lat.wait(Te)
                    for dc in range(DC):
                        mm = Te.matmul(self.PS5[:, :],
                                       WRG[:, wi * 2048 + dc * LAT + li * 128:
                                           wi * 2048 + dc * LAT + li * 128 + 128],
                                       AIN[:, dc * T + tb * 512: dc * T + tb * 512 + 512],
                                       start=(dc == 0), stop=(dc == DC - 1))
                    self.c_qk.inc(mm)
                    self.c_qk.wait(S)
                    ev = S.copy(LATB[:, (2 * wi + li) * T + tb * 512:
                                     (2 * wi + li) * T + tb * 512 + 512], self.PS5[:, :])
                    self.c_lat.inc(ev)

        if self.stop_at == "Blat":
            return
        self.c_qk.wait(P)
        self.c_wld.dinc(P.dma_start(
            out=WRG[:, 0:2048].rearrange("p (c m) -> p c m", m=D),
            in_=self.vuT[:, :].rearrange("(c p) m -> p c m", p=128)))
        wld_vu = self.c_wld.v
        self.c_av.wait(V)
        ms = V.memset(self.V65.ap()[:], 1.0)
        self.c_vev.inc(ms)
        Te.wait_ge(self.c_wld.h, wld_vu)
        self.c_lat.wait(Te)
        vev0 = self.c_vev.v
        for tt in range(16):
            ps = [self.PS0, self.PS1][tt % 2]
            if tt >= 2:
                self.c_vev.wait(Te, vev0 + (tt - 2) + 1)
            else:
                self.c_vev.wait(Te, vev0)
            for eb in range(2):
                for lc in range(2):
                    mm = Te.matmul(ps[:, eb * 512:(eb + 1) * 512],
                                   LATB[:, lc * T + tt * 128: lc * T + tt * 128 + 128],
                                   WRG[:, lc * D + eb * 512: lc * D + eb * 512 + 512],
                                   start=(lc == 0), stop=(lc == 1))
            self.c_sc.inc(mm)
            self.c_sc.wait(V)
            ev = V.tensor_copy(V65[:, tt, :, 0:64], ps[:].rearrange("p (h c) -> p h c", c=64))
            self.c_vev.inc(ev)

        self.c_u.wait(Y)
        self.c_nrm.wait(Y)
        self.c_ain.wait(Y)
        self.c_yld.dinc(Y.dma_start(out=ROT[:, 0:T], in_=self.cosqT[:, :]))
        self.c_yld.dinc(Y.dma_start(out=ROT[:, T:2 * T], in_=self.sinxqT[:, :]))
        yld_cos = self.c_yld.v
        V.wait_ge(self.c_yld.h, yld_cos)
        self.c_sc.wait(V)
        self.c_qk.wait(V)

        def rope_pass(kind, wbase, nsrc, rhs_of, pass_i):
            TMP = self.WRGB.ap()[:, 0:512]
            for g in range(4):
                for sub in range(4):
                    self.c_rope.wait(Te)
                    for ci in range(nsrc):
                        mm = Te.matmul(ps2[:, :],
                                       WRG[:, wbase + ci * NH * RD + g * 128:
                                           wbase + ci * NH * RD + g * 128 + 128],
                                       rhs_of(ci, sub),
                                       start=(ci == 0), stop=(ci == nsrc - 1))
                    self.c_rt2.inc(mm)
                    self.c_rt2.wait(V)
                    dst = RT[:, (kind * 4 + g) * T + sub * 512:
                             (kind * 4 + g) * T + sub * 512 + 512]
                    if pass_i == 0:
                        rp = V.tensor_tensor(dst, ps2[:, :],
                                             ROT[:, sub * 512: sub * 512 + 512], ALU.mult)
                    else:
                        V.tensor_tensor(TMP, ps2[:, :],
                                        ROT[:, T + sub * 512: T + sub * 512 + 512], ALU.mult)
                        rp = V.tensor_tensor(dst, dst, TMP, ALU.add)
                    self.c_rope.inc(rp)

        ps2 = self.PS2
        rhs_q = lambda ci, sb: LATB[:, (2 + ci) * T + sb * 512: (2 + ci) * T + sb * 512 + 512]
        rhs_k = lambda ci, sb: AIN[:, ci * T + sb * 512: ci * T + sb * 512 + 512]
        self.c_qk.wait(P)
        for wi, w_h in ((0, self.rqT), (1, self.rqTp)):
            self.c_wld.dinc(P.dma_start(
                out=WRG[:, 2048 + wi * 1024: 2048 + (wi + 1) * 1024]
                    .rearrange("p (c m) -> p c m", m=NH * RD),
                in_=w_h[:, :].rearrange("(c p) m -> p c m", p=128)))
        wld_rq = self.c_wld.v
        Te.wait_ge(self.c_wld.h, wld_rq)
        rope_pass(0, 2048, 2, rhs_q, 0)
        rope_pass(0, 3072, 2, rhs_q, 1)
        self.c_rope.wait(Y)
        self.c_yld.dinc(Y.dma_start(out=ROT[:, 0:T], in_=self.cosT[:, :]))
        self.c_yld.dinc(Y.dma_start(out=ROT[:, T:2 * T], in_=self.sinxT[:, :]))
        V.wait_ge(self.c_yld.h, self.c_yld.v)
        self.c_sc.wait(P)
        self.c_rt2.wait(P)
        for pass_i, w_h in ((0, self.rkT), (1, self.rkTp)):
            self.c_wld.dinc(P.dma_start(
                out=WRG[:, 0:4096].rearrange("p (c m) -> p c m", m=NH * RD),
                in_=w_h[:, :].rearrange("(c p) m -> p c m", p=128)))
            wld_rk = self.c_wld.v
            Te.wait_ge(self.c_wld.h, wld_rk)
            rope_pass(1, 0, DC, rhs_k, pass_i)
            if pass_i == 0:
                self.c_rt2.wait(P)
        del ps2

        def load_kq_slices_B(p, slot):
            self.c_rope.wait(P)
            ten = self._wrg_tenant[slot]
            if ten is not None:
                P.wait_ge(self.c_qk.h, ten)
            for wi, w_h in ((0, self.quT), (1, self.kuT)):
                dst = WRGB[:, (2 * slot + wi) * 256:(2 * slot + wi) * 256 + 256]
                self.c_wld.dinc(P.dma_start(
                    out=dst.rearrange("p (c m) -> p c m", m=128),
                    in_=w_h[:, p * 128:(p + 1) * 128].rearrange("(c p) m -> p c m", p=128)))

        def kq_prod_group_B(p, slot, wi, tb):
            ring = (2 * p + wi) % 4
            wsl = WRGB[:, (2 * slot + wi) * 256:(2 * slot + wi) * 256 + 256]
            lat_base = 2 * T if wi == 0 else 0
            self.c_qkev.wait(Te)
            for lc in range(2):
                mm = Te.matmul(self.PS5[:, :],
                               wsl[:, lc * 128:(lc + 1) * 128],
                               LATB[:, lat_base + lc * T + tb * 512:
                                    lat_base + lc * T + tb * 512 + 512],
                               start=(lc == 0), stop=(lc == 1))
            self.c_qk.inc(mm)
            self.c_qk.wait(S)
            ev = S.activation(QKR[:, ring * T + tb * 512: ring * T + tb * 512 + 512],
                              self.PS5[:, :], AF.Copy, scale=(0.125 if wi == 0 else 1.0))
            self.c_qkev.inc(ev)

        def rope_scatter(p):
            g = p // 2
            j0 = 2 * (p % 2)
            self.c_qkev.wait(P)
            self.c_rope.wait(P)
            for wi in range(2):
                ring = (2 * p + wi) % 4
                src_t = (wi * 4 + g) * T
                for j, dst0 in ((j0, 0), (j0 + 1, 64)):
                    cp = P.tensor_copy(
                        QKR[:, ring * T: ring * T + T][dst0:dst0 + RD, :],
                        RT[:, src_t: src_t + T][j * RD:(j + 1) * RD, :])
            self.c_scat.inc(cp)

        if self.stop_at == "Brope":
            return
        self.c_vev.wait(Te)
        load_kq_slices_B(0, 0)
        wld_for_pair = {0: self.c_wld.v}
        Te.wait_ge(self.c_wld.h, wld_for_pair[0])
        for wi in range(2):
            for tb in range(4):
                kq_prod_group_B(0, 0, wi, tb)
        self._wrg_tenant[0] = self.c_qk.v
        rope_scatter(0)
        scat_for_pair = {0: self.c_scat.v}
        if self.stop_at == "Bprod0":
            return

        for p in range(HP):
            if p + 1 < HP:
                load_kq_slices_B(p + 1, (p + 1) % 2)
                wld_for_pair[p + 1] = self.c_wld.v

            def interleave(qb, _p=p):
                if _p + 1 >= HP:
                    return
                for wi_tb in (2 * qb, 2 * qb + 1):
                    kq_prod_group_B(_p + 1, (_p + 1) % 2, wi_tb // 4, wi_tb % 4)
                if qb == 3:
                    rope_scatter(_p + 1)
                    scat_for_pair[_p + 1] = self.c_scat.v

            self._attention_pair(
                p, s, stage_b=True,
                qk_interleave=interleave if p + 1 < HP else None,
                qk_wait=(lambda eng, _p=p: eng.wait_ge(self.c_wld.h, wld_for_pair[_p + 1]))
                if p + 1 < HP else None,
                zdst=("y2", p),
                pre_wait=lambda eng, _p=p: eng.wait_ge(self.c_scat.h, scat_for_pair[_p]))
            if p + 1 < HP:
                self._wrg_tenant[(p + 1) % 2] = self.c_qk.v
            if self.stop_at == "Batt0" and p == int(_os.environ.get("MK_BP", "0")):
                self.c_zn.wait(P)
                self.c_cst.dinc(P.dma_start(out=self.dbg_y2[0:128, :],
                                            in_=self.AR_C.ap()[:, 0:T]))
                return

        if self.debug:
            self.c_zn.wait(P)
            for ec in range(DC):
                self.c_cst.dinc(P.dma_start(
                    out=self.dbg_y2[ec * 128:(ec + 1) * 128, :],
                    in_=self.AR_C.ap()[:, ec * T:(ec + 1) * T]))

        if self.stop_at == "B5dmp":
            return
        self._psum_quiesce()
        self.c_scat.wait(P)
        self.c_sc.wait(P)
        for ec in range(DC):
            self.c_uld.dinc(P.dma_start(
                out=self.AR_A.ap()[:, ec * D:(ec + 1) * D],
                in_=self.owT[ec * 128:(ec + 1) * 128, :]))
        uld_ow = self.c_uld.v
        if self.stop_at == "B5ow":
            return
        Te.wait_ge(self.c_uld.h, uld_ow)
        Y2 = self.AR_C.ap()
        OWB = self.AR_A.ap()
        self.c_rope.wait(Y)
        for dc in range(DC):
            cpar = [self.c_o0a, self.c_o0b][dc % 2]
            cpar.wait(Y)
            self.c_nrm.wait(Y)
            slot = ROT[:, (dc % 2) * T:(dc % 2) * T + T]
            ld = Y.dma_start(out=slot, in_=self.out0_scr[dc * 128:(dc + 1) * 128, :])
            self.c_yld.dinc(ld)
            yld_here = self.c_yld.v
            for nb in range(4):
                ps = [self.PS0, self.PS1][nb % 2]
                pcol = (nb % 2) * 0
                if nb >= 2:
                    self.c_nrm.wait(Te)
                for ec in range(DC):
                    mm = Te.matmul(ps[:, 0:512],
                                   OWB[:, ec * D + dc * 128: ec * D + dc * 128 + 128],
                                   Y2[:, ec * T + nb * 512: ec * T + nb * 512 + 512],
                                   start=(ec == 0), stop=(ec == DC - 1))
                self.c_oproj.inc(mm)
                self.c_oproj.wait(V)
                V.wait_ge(self.c_yld.h, yld_here)
                h1t = V.tensor_tensor(slot[:, nb * 512: nb * 512 + 512],
                                      slot[:, nb * 512: nb * 512 + 512],
                                      ps[:, 0:512], ALU.add)
                self.c_nrm.inc(h1t)
            self.c_nrm.wait(Y)
            cpar.dinc(Y.dma_start(out=self.h1_scr[dc * 128:(dc + 1) * 128, :], in_=slot))

        if self.stop_at == "B5oproj":
            return
        self.c_o0a.wait(P)
        self.c_o0b.wait(P)
        self._stats_pass(self.h1_scr)
        self._rows_from_stats()
        self.c_rowd.wait(P)
        self._row_to_f32r(self.ROWR(), self.rows_scr[3:4, :], T)
        self.c_rbc.wait(Te)
        self.c_stat.wait(Te)
        self.c_ain.wait(Te)
        for j in range(4):
            ps = [self.PS0, self.PS1][j // 2]
            mm = Te.matmul(ps[:, (j % 2) * 512:(j % 2) * 512 + 512], self.ONESR.ap()[:, :],
                           self.ROWR()[:, j * 512:(j + 1) * 512], start=True, stop=True)
        self.c_rbm.inc(mm)
        self.c_rbm.wait(P)
        self._row_to_f32r(self.ROWR(), self.rows_scr[4:5, :], T)
        self.c_rbc.wait(Te)
        for j in range(4):
            ps = [self.PS2, self.PS3, self.PS4, self.PS5][j]
            mm = Te.matmul(ps[:, :], self.ONESR.ap()[:, :],
                           self.ROWR()[:, j * 512:(j + 1) * 512], start=True, stop=True)
        self.c_rbm.inc(mm)
        if self.stop_at == "B5stats":
            return
        self.c_rowd.wait(Y)
        self.c_rbm.wait(V)
        self.c_o0a.wait(Y)
        self.c_o0b.wait(Y)
        MIN = self.AR_B.ap()
        for dc in range(DC):
            slot = ROT[:, (dc % 2) * T:(dc % 2) * T + T]
            if dc >= 2:
                self.c_ain.wait(Y, self._aindone[dc - 2])
            ld = Y.dma_start(out=slot, in_=self.h1_scr[dc * 128:(dc + 1) * 128, :])
            self.c_yld.dinc(ld)
            self.c_yld.wait(V)
            for j in range(2):
                V.tensor_tensor(slot[:, j * 1024:(j + 1) * 1024],
                                slot[:, j * 1024:(j + 1) * 1024],
                                [self.PS0, self.PS1][j][:, :], ALU.mult)
            for j in range(4):
                ps = [self.PS2, self.PS3, self.PS4, self.PS5][j]
                o0 = V.tensor_tensor(slot[:, j * 512:(j + 1) * 512],
                                     slot[:, j * 512:(j + 1) * 512], ps[:, :], ALU.add)
            self.vsync(o0)
            ai = V.tensor_scalar_mul(MIN[:, dc * T:(dc + 1) * T], slot,
                                     self.WCOL.ap()[:, 8 + dc:8 + dc + 1])
            self.c_ain.inc(ai)
            self._aindone[dc] = self.c_ain.v
        if self.debug:
            self.c_ain.wait(P)
            for dc in range(DC):
                self.c_cst.dinc(P.dma_start(
                    out=self.dbg_min[dc * 128:(dc + 1) * 128, :],
                    in_=MIN[:, dc * T:(dc + 1) * T]))

    def phase_C(self, s):
        nc, Te, V, S, P, Y = self.nc, self.Te, self.V, self.S, self.P, self.Y
        MIN = self.AR_B.ap()
        RTR = self.RTR.ap()
        lg = RTR[:, 0:8]
        m8 = RTR[:, 8:16]
        zz = RTR[:, 16:24]
        ind = RTR[:, 24:32]
        probs = RTR[:, 32:40]
        pw = RTR[:, 40:48]
        den = RTR[:, 48:49]
        rden = RTR[:, 49:50]
        wgt = RTR[:, 56:64]
        stg = RTR[:, 96:224]
        self._psum_quiesce()
        self.c_cst.wait(Te)
        for tt in range(16):
            ps = [self.PS2, self.PS3][tt % 2]
            if tt >= 2:
                Te.wait_ge(self.c_vx.h, self._rtr_t[tt % 2])
            for dc in range(DC):
                Te.matmul(ps[:, 0:8],
                          MIN[:, dc * T + tt * 128: dc * T + tt * 128 + 128],
                          self.RW8.ap()[:, dc * 8:(dc + 1) * 8],
                          start=(dc == 0), stop=False)
            mm = Te.matmul(ps[:, 0:8], self.ONESR.ap()[:, :], self.BIAS8r(),
                           start=False, stop=True)
            self.c_rtr.inc(mm)
            self.c_rtr.wait(V)
            self.vsync(V.tensor_copy(lg, ps[:, 0:8]))
            self._rtr_t[tt % 2] = self.c_vx.v
            self.vsync(V.memset(RTR[:, 7:8], NEG))
            self.c_vx.wait(S)
            self.ssync(S.activation(probs, lg, AF.Sigmoid))
            self.vsync(V.max(out=m8, in_=lg))
            self.vsync(V.memset(RTR[:, 10:16], NEG))
            self.vsync(V.match_replace(out=zz, in_to_replace=m8, in_values=lg,
                                       imm_value=NEG))
            self.vsync(V.tensor_sub(ind, lg, zz))
            self.vsync(V.tensor_scalar_min(ind, ind, 1.0))
            self.c_sx.wait(V)
            self.vsync(V.tensor_mul(pw, probs, ind))
            self.vsync(V.reduce_sum(den, pw, axis=AX.X))
            self.vsync(V.reciprocal(rden, den))
            self.vsync(V.tensor_scalar_mul(wgt, pw, rden))
            self.c_vx.wait(Te)
            self.c_rw.wait(Te)
            tr = Te.transpose(self.PS4[0:8, 0:128], wgt, self.IDF.ap()[:, :])
            self.c_eo.inc(tr)
            self.c_eo.wait(V)
            cpw = V.tensor_copy(stg[0:8, :], self.PS4[0:8, 0:128])
            self.c_rw.inc(cpw)
            self.c_rw.wait(Y)
            self.c_misc.dinc(Y.dma_start(out=self.wgt_scr[:, tt * 128:(tt + 1) * 128],
                                         in_=stg[0:8, :]))

    def phase_D(self, s):
        nc, Te, V, S, P, Y = self.nc, self.Te, self.V, self.S, self.P, self.Y
        MIN = self.AR_B.ap()
        ACT = self.AR_A.ap()
        SH = self.AR_C.ap().bitcast(F32)
        ROT = self.ROT.ap()
        WGB = self.MSK.ap().bitcast(F32)
        GU = self.V65.ap()
        self._psum_quiesce()
        self.c_av.wait(Y)
        self.c_zn.wait(Y)
        self.c_oproj.wait(Y)
        self.c_ain.wait(Y)
        for half in range(2):
            col0 = half * 1024
            for hc in range(DC):
                self.c_cmb.wait(Y)
                ld1 = Y.dma_start(out=SH[:, hc * 1024:(hc + 1) * 1024],
                                  in_=self.h1_scr[hc * 128:(hc + 1) * 128, col0:col0 + 1024])
                self.c_sld.dinc(ld1)
                ld2 = Y.dma_start(out=ROT[:, (hc % 2) * T:(hc % 2) * T + 1024],
                                  in_=self.out0_scr[hc * 128:(hc + 1) * 128, col0:col0 + 1024])
                self.c_sld.dinc(ld2)
                self.c_sld.wait(V)
                ad = V.tensor_tensor(SH[:, hc * 1024:(hc + 1) * 1024],
                                     SH[:, hc * 1024:(hc + 1) * 1024],
                                     ROT[:, (hc % 2) * T:(hc % 2) * T + 1024], ALU.add)
                self.c_cmb.inc(ad)
            for e in range(NE):
                if e > 0:
                    self.c_rbm.wait(P)
                    self.c_misc.wait(P)
                    self._row_to_f32r(self.ROWR(), self.wgt_scr[e - 1:e, col0:col0 + 1024], 1024)
                    self.c_rbc.wait(Te)
                    self.c_wbc.wait(Te)
                    for j in range(2):
                        ps = [self.PS2, self.PS3][j]
                        mm = Te.matmul(ps[:, :], self.ONESR.ap()[:, :],
                                       self.ROWR()[:, j * 512:(j + 1) * 512],
                                       start=True, stop=True)
                    self.c_rbm.inc(mm)
                    self.c_rbm.wait(V)
                    V.tensor_copy(WGB[:, 0:512], self.PS2[:, :])
                    wb = V.tensor_copy(WGB[:, 512:1024], self.PS3[:, :])
                    self.c_wbc.inc(wb)
                def gu_load(it):
                    slot = it % 4
                    ten = self._gu_tenant[slot]
                    if ten is not None:
                        Y.wait_ge(self.c_gmm.h, ten)
                    gl = Y.dma_start(
                        out=GU[:, slot * 2048: slot * 2048 + 2048]
                            .rearrange("p (c m) -> p c m", m=256),
                        in_=self.gu_bf[e, :, it * 256:(it + 1) * 256]
                            .rearrange("(c p) m -> p c m", p=128))
                    self.c_gld.dinc(gl)
                    return self.c_gld.v

                gld_for = {0: gu_load(0)}
                for it in range(IT):
                    slot = it % 4
                    if it + 1 < IT:
                        gld_for[it + 1] = gu_load(it + 1)
                    Te.wait_ge(self.c_gld.h, gld_for[it])
                    if it >= 1:
                        self.c_silu.wait(Te)
                    for nb in range(2):
                        for dc in range(DC):
                            mm = Te.matmul(
                                self.PS0[:, nb * 512: nb * 512 + 512],
                                GU[:, slot * 2048 + dc * 256:
                                    slot * 2048 + dc * 256 + 128],
                                MIN[:, dc * T + col0 + nb * 512:
                                    dc * T + col0 + nb * 512 + 512],
                                start=(dc == 0), stop=(dc == DC - 1))
                    self.c_gat.inc(mm)
                    if it >= 1:
                        self.c_act.wait(Te)
                    for nb in range(2):
                        for dc in range(DC):
                            mm = Te.matmul(
                                self.PS1[:, nb * 512: nb * 512 + 512],
                                GU[:, slot * 2048 + dc * 256 + 128:
                                    slot * 2048 + dc * 256 + 256],
                                MIN[:, dc * T + col0 + nb * 512:
                                    dc * T + col0 + nb * 512 + 512],
                                start=(dc == 0), stop=(dc == DC - 1))
                    self.c_gmm.inc(mm)
                    self._gu_tenant[slot] = self.c_gmm.v
                    self.c_gat.wait(S)
                    self.c_act.wait(S)
                    sg = S.activation(ROT[:, (it % 2) * T:(it % 2) * T + 1024],
                                      self.PS0[:, :], AF.Silu)
                    self.c_silu.inc(sg)
                    self.c_silu.wait(V)
                    self.c_gmm.wait(V)
                    ac = V.tensor_tensor(ACT[:, it * 1024:(it + 1) * 1024],
                                         ROT[:, (it % 2) * T:(it % 2) * T + 1024],
                                         self.PS1[:, :], ALU.mult)
                    self.c_act.inc(ac)
                def dn_load(ht):
                    slot = ht % 4
                    ten = self._dn_tenant[slot]
                    if ten is not None:
                        Y.wait_ge(self.c_eo.h, ten)
                    dl = Y.dma_start(
                        out=GU[:, 8192 + slot * 2048: 8192 + slot * 2048 + 2048]
                            .rearrange("p (c m) -> p c m", m=128),
                        in_=self.dn_bf[e, :, ht * 128:(ht + 1) * 128]
                            .rearrange("(c p) m -> p c m", p=128))
                    self.c_dld.dinc(dl)
                    return self.c_dld.v

                dld_for = {0: dn_load(0)}
                for ht in range(DC):
                    slot = ht % 4
                    if ht + 1 < DC:
                        dld_for[ht + 1] = dn_load(ht + 1)
                    Te.wait_ge(self.c_dld.h, dld_for[ht])
                    self.c_act.wait(Te)
                    for nb in range(2):
                        ps = [self.PS4, self.PS5][nb]
                        if ht >= 1:
                            self.c_cmb.wait(Te, self._eo_t[nb])
                        for ic in range(IT):
                            mm = Te.matmul(ps[:, :],
                                           GU[:, 8192 + slot * 2048 + ic * 128:
                                               8192 + slot * 2048 + ic * 128 + 128],
                                           ACT[:, ic * 1024 + nb * 512:
                                               ic * 1024 + nb * 512 + 512],
                                           start=(ic == 0), stop=(ic == IT - 1))
                        self.c_eo.inc(mm)
                        self._eo_t[nb] = None
                        self.c_eo.wait(V)
                        hc_col = ht * 1024 + nb * 512
                        if e == 0:
                            cb = V.tensor_tensor(SH[:, hc_col:hc_col + 512],
                                                 SH[:, hc_col:hc_col + 512],
                                                 ps[:, :], ALU.add)
                            self.c_cmb.inc(cb)
                        else:
                            self.c_wbc.wait(V)
                            V.tensor_tensor(ROT[:, 1024 + nb * 512: 1024 + nb * 512 + 512],
                                            ps[:, :], WGB[:, nb * 512:(nb + 1) * 512], ALU.mult)
                            cb = V.tensor_tensor(SH[:, hc_col:hc_col + 512],
                                                 SH[:, hc_col:hc_col + 512],
                                                 ROT[:, 1024 + nb * 512: 1024 + nb * 512 + 512],
                                                 ALU.add)
                            self.c_cmb.inc(cb)
                        self._eo_t[nb] = self.c_cmb.v
                    self._dn_tenant[slot] = self.c_eo.v
            self.c_cmb.wait(Y)
            for hc in range(DC):
                st = Y.dma_start(out=self.s_scr[hc * 128:(hc + 1) * 128, col0:col0 + 1024],
                                 in_=SH[:, hc * 1024:(hc + 1) * 1024])
                self.c_sdma.dinc(st)
            self.c_sdma.wait(Y)

    def phase_E(self, s):
        nc, Te, V, S, P, Y = self.nc, self.Te, self.V, self.S, self.P, self.Y
        ROT = self.ROT.ap()
        self._psum_quiesce()
        self.c_sdma.wait(P)
        self.c_silu.wait(Y)
        self.c_cmb.wait(Y)
        self._stats_pass(self.s_scr)
        self._rows_from_stats()
        self.c_rowd.wait(P)
        self._row_to_f32r(self.ROWR(), self.rows_scr[3:4, :], T)
        self.c_rbc.wait(Te)
        self.c_stat.wait(Te)
        for j in range(4):
            ps = [self.PS0, self.PS1][j // 2]
            mm = Te.matmul(ps[:, (j % 2) * 512:(j % 2) * 512 + 512], self.ONESR.ap()[:, :],
                           self.ROWR()[:, j * 512:(j + 1) * 512], start=True, stop=True)
        self.c_rbm.inc(mm)
        self.c_rbm.wait(P)
        self._row_to_f32r(self.ROWR(), self.rows_scr[4:5, :], T)
        self.c_rbc.wait(Te)
        for j in range(4):
            ps = [self.PS2, self.PS3, self.PS4, self.PS5][j]
            mm = Te.matmul(ps[:, :], self.ONESR.ap()[:, :],
                           self.ROWR()[:, j * 512:(j + 1) * 512], start=True, stop=True)
        self.c_rbm.inc(mm)
        self.c_rbm.wait(V)
        self.c_rowd.wait(Y)
        for dc in range(DC):
            cpar = [self.c_o0a, self.c_o0b][dc % 2]
            slot = ROT[:, (dc % 2) * T:(dc % 2) * T + T]
            cpar.wait(Y)
            self.c_nrm.wait(Y)
            ld = Y.dma_start(out=slot, in_=self.s_scr[dc * 128:(dc + 1) * 128, :])
            self.c_yld.dinc(ld)
            self.c_yld.wait(V)
            for j in range(2):
                V.tensor_tensor(slot[:, j * 1024:(j + 1) * 1024],
                                slot[:, j * 1024:(j + 1) * 1024],
                                [self.PS0, self.PS1][j][:, :], ALU.mult)
            for j in range(4):
                ps = [self.PS2, self.PS3, self.PS4, self.PS5][j]
                o0 = V.tensor_tensor(slot[:, j * 512:(j + 1) * 512],
                                     slot[:, j * 512:(j + 1) * 512], ps[:, :], ALU.add)
            self.c_nrm.inc(o0)
            self.c_nrm.wait(Y)
            cpar.dinc(Y.dma_start(out=self.out_t[s, dc * 128:(dc + 1) * 128, :], in_=slot))

    def _reset_seq_state(self):
        self._sc_tenant = [None, None]
        self._pr_tenant = [None, None, None, None]
        self._ynorm_pending = None
        self._zst_tenant = [None, None]
        self._rot_tenant = [None, None]
        self._wrg_tenant = [None, None]
        self._o0dma = {}
        self._aindone = {}
        self._rtr_t = [None, None]
        self._gu_tenant = [None, None, None, None]
        self._dn_tenant = [None, None, None, None]
        self._eo_t = [None, None]

    def _seq_barrier(self):
        for eng in (self.Te, self.V, self.S, self.P, self.Y):
            for c in vars(self).values():
                if isinstance(c, Ctr) and c.v > 0:
                    c.wait(eng)

    def build(self):
        self.prep_and_consts()
        if self.stop_at == "prep":
            self.finish()
            return
        for s in range(self.n_seq):
            if s > 0:
                self._seq_barrier()
            self._reset_seq_state()
            self.phase_A(s)
            if self.last_phase == "A" or self.stop_at in ("load", "vprod", "attn", "xt"):
                break
            self.phase_B(s)
            if self.last_phase == "B" or self.stop_at is not None:
                break
            self.phase_C(s)
            if self.last_phase == "C":
                break
            self.phase_D(s)
            if self.last_phase == "D":
                break
            self.phase_E(s)
        self.finish()

    def finish(self):
        for c in vars(self).values():
            if isinstance(c, Ctr) and c.v > 0:
                c.wait(self.Y)


def kernel(**inputs):
    in_maps = build_host_inputs(inputs)
    nc = bass.Bass("TRN2", target_bir_lowering=False)
    mk = MK(nc)
    mk.build()
    res = run_bass_kernel_spmd(nc, in_maps, core_ids=list(range(NCORE)))
    outs = []
    for c in range(NCORE):
        o = res.results[c]["out_t"]
        outs.append(np.ascontiguousarray(o.transpose(0, 2, 1)))
    return np.concatenate(outs, 0).astype(np.float32)


# version 04
# speedup vs baseline: 1.0108x; 1.0108x over previous
import os as _os
import numpy as np
import ml_dtypes

import concourse.bass as bass
import concourse.mybir as mybir
from concourse.bass_utils import run_bass_kernel_spmd

F32 = mybir.dt.float32
F32R = mybir.dt.float32r
BF16 = mybir.dt.bfloat16
AF = mybir.ActivationFunctionType
ALU = mybir.AluOpType
AX = mybir.AxisListType

NCORE = 8
NSEQ = 2
T = 2048
D = 1024
DC = D // 128
NH = 16
HD = 64
HP = NH // 2
LAT = 256
RD = 32
I_FF = 2048
IT = I_FF // 128
NE = 8
EPS = 1e-5
NEG = -1.0e30


class Ctr:
    def __init__(self, nc, name):
        self.h = nc.alloc_semaphore(name)
        self.v = 0

    def inc(self, ins, k=1):
        ins.then_inc(self.h, k)
        self.v += k
        return ins

    def dinc(self, ins):
        return self.inc(ins, 16)

    def wait(self, eng, v=None):
        eng.wait_ge(self.h, self.v if v is None else v)


def build_host_inputs(inputs):
    f = np.float32
    A = lambda a: np.ascontiguousarray(np.asarray(a, f))
    g = inputs
    xt = A(np.asarray(g["L0_attn_x"], f).transpose(0, 2, 1))
    wqkv = A(g["L0_attn_kernel"])
    kvdT = A(np.asarray(g["L0_ffn_kv_d"], f).T)
    qdT = A(np.asarray(g["L0_ffn_q_d"], f).T)
    kuT = A(np.asarray(g["L0_ffn_k_u"], f).T)
    quT = A(np.asarray(g["L0_ffn_q_u"], f).T)
    vuT = A(np.asarray(g["L0_ffn_v_u"], f).T)
    rqT = A(np.asarray(g["L0_ffn_rope_q_w"], f).T)
    rkT = A(np.asarray(g["L0_ffn_rope_k_w"], f).T)
    perm = np.arange(NH * RD)
    i = perm % RD
    perm = (perm - i) + ((i + 16) % RD)
    rqTp = A(rqT[:, perm])
    rkTp = A(rkT[:, perm])
    owT = A(np.asarray(g["L0_ffn_o_w"], f).T)
    rwT8 = np.zeros((D, 8), f)
    rwT8[:, :7] = np.asarray(g["L0_ffn_router_w"], f).T
    rbias8 = np.zeros((1, 8), f)
    rbias8[0, :7] = np.asarray(g["L0_ffn_routing_bias"], f)
    ln1w = A(np.asarray(g["L0_ffn_ln1_w"], f).reshape(1, D))
    ln2w = A(np.asarray(g["L0_ffn_ln2_w"], f).reshape(1, D))

    gates = np.concatenate([np.asarray(g["L0_ffn_shared_gate"], f),
                            np.asarray(g["L0_ffn_routed_gate"], f)], 0)
    ups = np.concatenate([np.asarray(g["L0_ffn_shared_up"], f),
                          np.asarray(g["L0_ffn_routed_up"], f)], 0)
    downs = np.concatenate([np.asarray(g["L0_ffn_shared_down"], f),
                            np.asarray(g["L0_ffn_routed_down"], f)], 0)
    guT = np.empty((NE, D, 2 * I_FF), f)
    g4 = guT.reshape(NE, D, IT, 256)
    g4[:, :, :, 0:128] = gates.transpose(0, 2, 1).reshape(NE, D, IT, 128)
    g4[:, :, :, 128:256] = ups.transpose(0, 2, 1).reshape(NE, D, IT, 128)
    dnT = A(downs.transpose(0, 2, 1))

    invf = 1.0 / (10000.0 ** (np.arange(0, RD, 2, dtype=np.float64) / RD))
    fr = np.arange(T, dtype=np.float64)[None, :] * invf[:, None]
    cos32 = np.cos(np.concatenate([fr, fr], 0))
    sin32 = np.sin(np.concatenate([fr, fr], 0))
    sinx32 = sin32.copy()
    sinx32[:16] *= -1.0
    cosT = A(np.tile(cos32, (4, 1)))
    sinxT = A(np.tile(sinx32, (4, 1)))
    cosqT = A(cosT * 0.125)
    sinxqT = A(sinxT * 0.125)
    kk = np.arange(128)[:, None]
    qq = np.arange(512)[None, :]
    masks = np.ascontiguousarray(
        np.stack([(kk + j * 128 <= qq) for j in range(4)]).astype(ml_dtypes.bfloat16))
    ones1 = np.ones((1, 128), f)

    common = dict(wqkv=wqkv, kvdT=kvdT, qdT=qdT, kuT=kuT, quT=quT, vuT=vuT,
                  rqT=rqT, rqTp=rqTp, rkT=rkT, rkTp=rkTp, owT=owT,
                  rwT8=rwT8, rbias8=rbias8, ln1w=ln1w, ln2w=ln2w,
                  guT=np.ascontiguousarray(guT), dnT=dnT, cosT=cosT, sinxT=sinxT,
                  cosqT=cosqT, sinxqT=sinxqT,
                  masks=masks, ones1=ones1)
    in_maps = []
    for c in range(NCORE):
        m = dict(common)
        m["xt"] = np.ascontiguousarray(xt[c * NSEQ:(c + 1) * NSEQ])
        in_maps.append(m)
    return in_maps


class MK:
    def __init__(self, nc, debug=False, n_seq=NSEQ, last_phase="E", stop_at=None,
                 skip_prep=False):
        self.nc = nc
        self.debug = debug
        self.n_seq = n_seq
        self.last_phase = last_phase
        self.stop_at = stop_at
        self.skip_prep = skip_prep
        self.Te, self.V, self.S, self.P, self.Y = nc.tensor, nc.vector, nc.scalar, nc.gpsimd, nc.sync
        self._declare_hbm()
        self._alloc_sbuf()
        self._alloc_psum()
        self._alloc_sems()
        self.out_dma_waits = []

    def _declare_hbm(self):
        nc = self.nc
        d = nc.dram_tensor
        self.xt = d("xt", [NSEQ, D, T], F32, kind="ExternalInput").ap()
        self.wqkv = d("wqkv", [3, D, D], F32, kind="ExternalInput").ap()
        self.kvdT = d("kvdT", [D, LAT], F32, kind="ExternalInput").ap()
        self.qdT = d("qdT", [D, LAT], F32, kind="ExternalInput").ap()
        self.kuT = d("kuT", [LAT, D], F32, kind="ExternalInput").ap()
        self.quT = d("quT", [LAT, D], F32, kind="ExternalInput").ap()
        self.vuT = d("vuT", [LAT, D], F32, kind="ExternalInput").ap()
        self.rqT = d("rqT", [LAT, NH * RD], F32, kind="ExternalInput").ap()
        self.rqTp = d("rqTp", [LAT, NH * RD], F32, kind="ExternalInput").ap()
        self.rkT = d("rkT", [D, NH * RD], F32, kind="ExternalInput").ap()
        self.rkTp = d("rkTp", [D, NH * RD], F32, kind="ExternalInput").ap()
        self.owT = d("owT", [D, D], F32, kind="ExternalInput").ap()
        self.rwT8 = d("rwT8", [D, 8], F32, kind="ExternalInput").ap()
        self.rbias8 = d("rbias8", [1, 8], F32, kind="ExternalInput").ap()
        self.ln1w = d("ln1w", [1, D], F32, kind="ExternalInput").ap()
        self.ln2w = d("ln2w", [1, D], F32, kind="ExternalInput").ap()
        self.guT = d("guT", [NE, D, 2 * I_FF], F32, kind="ExternalInput").ap()
        self.dnT = d("dnT", [NE, I_FF, D], F32, kind="ExternalInput").ap()
        self.cosT = d("cosT", [128, T], F32, kind="ExternalInput").ap()
        self.sinxT = d("sinxT", [128, T], F32, kind="ExternalInput").ap()
        self.cosqT = d("cosqT", [128, T], F32, kind="ExternalInput").ap()
        self.sinxqT = d("sinxqT", [128, T], F32, kind="ExternalInput").ap()
        self.masks = d("masks", [4, 128, 512], BF16, kind="ExternalInput").ap()
        self.ones1 = d("ones1", [1, 128], F32, kind="ExternalInput").ap()

        self.out_t = d("out_t", [NSEQ, D, T], F32, kind="ExternalOutput").ap()
        scr = "ExternalOutput" if self.debug else "Internal"
        self.gu_bf = d("gu_bf", [NE, D, 2 * I_FF], BF16, kind="Internal").ap()
        self.dn_bf = d("dn_bf", [NE, I_FF, D], BF16, kind="Internal").ap()
        self.u_scr = d("u_scr", [D, T], F32, kind=scr).ap()
        self.out0_scr = d("out0_scr", [D, T], F32, kind=scr).ap()
        self.h1_scr = d("h1_scr", [D, T], F32, kind=scr).ap()
        self.s_scr = d("s_scr", [D, T], F32, kind=scr).ap()
        self.rows_scr = d("rows_scr", [8, T], F32, kind=scr).ap()
        self.wgt_scr = d("wgt_scr", [8, T], F32, kind=scr).ap()
        if self.debug:
            self.dbg_ain = d("dbg_ain", [D, T], F32, kind="ExternalOutput").ap()
            self.dbg_min = d("dbg_min", [D, T], F32, kind="ExternalOutput").ap()
            self.dbg_y2 = d("dbg_y2", [D, T], F32, kind="ExternalOutput").ap()
            self.dbg_wgt = d("dbg_wgt", [8, T], F32, kind="ExternalOutput").ap()
            self.dbg_rowt = d("dbg_rowt", [128, 96], F32, kind="ExternalOutput").ap()

    def _alloc_sbuf(self):
        a = self.nc.alloc_sbuf_tensor
        self.AR_A = a("AR_A", [128, 8 * T], BF16)
        self.AR_B = a("AR_B", [128, 8 * T], BF16)
        self.AR_C = a("AR_C", [128, 8 * T], BF16)
        self.V65 = a("V65", [128, 16 * NH * 65], BF16)
        self.QKR = a("QKR", [128, 4 * T], BF16)
        self.LATB = a("LATB", [128, 4 * T], BF16)
        self.ROT = a("ROT", [128, 2 * T], F32)
        self.WRG = a("WRG", [128, 4 * D], BF16)
        self.WRGB = a("WRGB", [128, 1024], BF16)
        self.MSK = a("MSK", [128, 4 * 512], BF16)
        self.ONESR = a("ONESR", [1, 128], F32R)
        self.ONESB = a("ONESB", [128, 1], BF16)
        self.ROWT = a("ROWT", [128, 10 * 16], F32)
        self.ROWB = a("ROWB", [128, 3592], F32R)
        self.RTR = a("RTR", [128, 224], F32)
        self.RW8 = a("RW8", [128, 8 * 8], BF16)
        self.WCOL = a("WCOL", [128, 16], F32)
        self.IDF = a("IDF", [128, 128], F32)
        rem = self.nc.sbuf_bytes_remaining
        assert rem >= 0, f"SBUF over budget by {-rem} bytes"

    def rt_mu(self):    return self.ROWT.ap()[:, 0:16]
    def rt_msq(self):   return self.ROWT.ap()[:, 16:32]
    def rt_var(self):   return self.ROWT.ap()[:, 32:48]
    def rt_rstd(self):  return self.ROWT.ap()[:, 48:64]
    def rt_r2(self):    return self.ROWT.ap()[:, 64:80]
    def rt_tmp(self):   return self.ROWT.ap()[:, 80:96]
    def rt_n1(self):    return self.ROWT.ap()[:, 96:112]
    def rt_n2(self):    return self.ROWT.ap()[:, 112:128]
    def rs_mu(self):    return self.ROT.ap()[0:1, 0:T]
    def rs_msq(self):   return self.ROT.ap()[0:1, T:2 * T]
    def ROWR(self):     return self.ROWB.ap()[0:1, 0:T]
    def WROW(self):     return self.ROWB.ap()[0:1, T:T + D]
    def RRrow(self):    return self.ROWB.ap()[0:1, T + D:T + D + 512]
    def BIAS8r(self):   return self.ROWB.ap()[0:1, T + D + 512:T + D + 520]
    def URt(self):      return self.QKR.ap()[:, 0:T]
    def SQRt(self):     return self.QKR.ap()[:, T:2 * T]
    def WGTrows(self):  return self.LATB.ap().bitcast(F32)[0:8, 0:T]

    def _alloc_psum(self):
        p = self.nc.alloc_psum_tensor
        self.PS0 = p("PS0", [128, 1024], F32)
        self.PS1 = p("PS1", [128, 1024], F32)
        self.PS2 = p("PS2", [128, 512], F32)
        self.PS3 = p("PS3", [128, 512], F32)
        self.PS4 = p("PS4", [128, 512], F32)
        self.PS5 = p("PS5", [128, 512], F32)

    def _alloc_sems(self):
        for n in ["c_pld", "c_pst", "c_xt", "c_wld", "c_vev", "c_qk", "c_qkev",
                  "c_sc", "c_exp", "c_msk", "c_av", "c_yev", "c_rcp", "c_rb",
                  "c_zn", "c_xld", "c_u", "c_udma",
                  "c_uld", "c_sq", "c_stat", "c_rows", "c_rbc", "c_nrm",
                  "c_o0", "c_ain", "c_lat", "c_rope", "c_scat",
                  "c_oproj", "c_h1", "c_sdma", "c_hld",
                  "c_rl", "c_rv", "c_rw", "c_rt2", "c_rtr",
                  "c_gld", "c_gmm", "c_gat", "c_silu", "c_act", "c_dld", "c_eo", "c_cmb",
                  "c_sld", "c_wbc", "c_out", "c_misc", "c_vx", "c_pl", "c_ev2",
                  "c_cst", "c_yld", "c_rowd", "c_rbm",
                  "c_ua", "c_ub", "c_o0a", "c_o0b", "c_sx"]:
            setattr(self, n, Ctr(self.nc, n))

    def rsqrt_rows(self, out_ap, v_ap):
        V, S = self.V, self.S
        n1, n2 = self.rt_n1(), self.rt_n2()
        self.c_vx.wait(S)
        self.ssync(S.activation(n1, v_ap, AF.Ln))
        self.ssync(S.activation(out_ap, n1, AF.Exp, scale=-0.5))
        self.c_sx.wait(V)
        for _ in range(2):
            self.vsync(V.tensor_tensor(n1, out_ap, out_ap, ALU.mult))
            self.vsync(V.tensor_tensor(n2, v_ap, n1, ALU.mult))
            self.vsync(V.tensor_scalar(n1, n2, -0.5, 1.5, ALU.mult, ALU.add))
            yy = self.vsync(V.tensor_tensor(out_ap, out_ap, n1, ALU.mult))
        return yy

    def ssync(self, ins):
        self.c_sx.inc(ins)
        self.c_sx.wait(self.S)
        return ins

    def vsync(self, ins):
        self.c_vx.inc(ins)
        self.c_vx.wait(self.V)
        return ins

    def prep_and_consts(self):
        P, Y = self.P, self.Y
        self.c_misc.dinc(Y.dma_start(
            out=self.MSK.ap()[:, :].rearrange("p (j n) -> p j n", j=4),
            in_=self.masks[:, :, :].rearrange("j p n -> p j n")))
        self._msk_ld = self.c_misc.v
        self.c_cst.dinc(P.dma_start(out=self.ONESR.ap()[:, :], in_=self.ones1[:, :]))
        self.c_cst.dinc(P.dma_start(
            out=self.ONESB.ap()[:, :], in_=self.ones1[:, :].rearrange("a b -> b a")))
        self.c_cst.dinc(P.dma_start(out=self.BIAS8r(), in_=self.rbias8[:, :]))
        self.c_cst.dinc(P.dma_start(
            out=self.RW8.ap()[:, :].rearrange("p (c e) -> p c e", e=8),
            in_=self.rwT8[:, :].rearrange("(c p) e -> p c e", p=128)))
        self.c_cst.dinc(P.dma_start(
            out=self.WCOL.ap()[:, 0:8],
            in_=self.ln1w[:, :].rearrange("a (c p) -> (a p) c", p=128),
            allow_slow_non_contiguous=True))
        self.c_cst.dinc(P.dma_start(
            out=self.WCOL.ap()[:, 8:16],
            in_=self.ln2w[:, :].rearrange("a (c p) -> (a p) c", p=128),
            allow_slow_non_contiguous=True))
        idn = self.nc.inline_tensor(np.eye(128, dtype=np.float32), "ident128")
        self.c_misc.dinc(Y.dma_start(out=self.IDF.ap()[:, :], in_=idn.ap()[:, :]))

        if self.skip_prep:
            self.prep_sts = 0
            self._prep_jobs = []
            return
        jobs = []
        for e in range(NE):
            for r in range(DC):
                jobs.append(("gu", e, r))
        for e in range(NE):
            for r in range(0, IT, 4):
                jobs.append(("dn", e, r))
        self._prep_jobs = jobs
        self._prep_n = 0
        self._prep_store_done = {}
        self.prep_sts = len(jobs) * 16
        if _os.environ.get("MK_PREP_UPFRONT"):
            self._emit_prep(len(jobs))

    def _emit_prep(self, k):
        P, Y = self.P, self.Y
        stage = self.AR_B.ap()
        while k > 0 and self._prep_jobs:
            kind, e, r = self._prep_jobs.pop(0)
            n = self._prep_n
            slot = stage[:, (n % 2) * 8192:(n % 2) * 8192 + 4096]
            if n >= 2:
                P.wait_ge(self.c_pst.h, self._prep_store_done[n - 2])
            if kind == "gu":
                self.c_pld.dinc(P.dma_start(out=slot, in_=self.guT[e, r * 128:(r + 1) * 128, :]))
                Y.wait_ge(self.c_pld.h, self.c_pld.v)
                self.c_pst.dinc(Y.dma_start(out=self.gu_bf[e, r * 128:(r + 1) * 128, :], in_=slot))
            else:
                src = self.dnT[e, r * 128:(r + 4) * 128, :].rearrange("(a p) n -> p a n", p=128)
                dst = self.dn_bf[e, r * 128:(r + 4) * 128, :].rearrange("(a p) n -> p a n", p=128)
                slot3 = slot.rearrange("p (a n) -> p a n", a=4)
                self.c_pld.dinc(P.dma_start(out=slot3, in_=src))
                Y.wait_ge(self.c_pld.h, self.c_pld.v)
                self.c_pst.dinc(Y.dma_start(out=dst, in_=slot3))
            self._prep_store_done[n] = self.c_pst.v
            self._prep_n = n + 1
            k -= 1

    def phase_A(self, s):
        nc, Te, V, S, P, Y = self.nc, self.Te, self.V, self.S, self.P, self.Y
        XT = self.AR_A.ap()
        WV = self.LATB.ap()
        V65 = self.V65.ap().rearrange("p (k h c) -> p k h c", k=16, h=NH)
        QKR = self.QKR.ap()
        ZST = self.AR_C.ap().bitcast(F32)
        PR = self.AR_C.ap()
        PR_OFF = 8192
        ROT = self.ROT.ap()
        WRG = self.WRG.ap()

        for dc in range(DC):
            self.c_xt.dinc(P.dma_start(out=XT[:, dc * T:(dc + 1) * T],
                                       in_=self.xt[s, dc * 128:(dc + 1) * 128, :]))
        for dc in range(DC):
            self.c_wld.dinc(P.dma_start(
                out=WV[:, dc * D:(dc + 1) * D],
                in_=self.wqkv[2, dc * 128:(dc + 1) * 128, :]))
        if self.stop_at == "xt":
            return
        self.c_xt.wait(V)
        ms = V.memset(self.V65.ap()[:], 1.0)
        self.c_vev.inc(ms)

        if self.stop_at == "load":
            return
        xt_ready = self.c_xt.v
        wv_ready = self.c_wld.v
        Te.wait_ge(self.c_xt.h, xt_ready)
        Te.wait_ge(self.c_wld.h, wv_ready)
        vev_done = self.c_vev.v
        for tt in range(16):
            ps = [self.PS0, self.PS1][tt % 2]
            if tt >= 2:
                self.c_vev.wait(Te, vev_done + (tt - 2) + 1)
            for eb in range(2):
                for dc in range(DC):
                    mm = Te.matmul(ps[:, eb * 512:(eb + 1) * 512],
                                   XT[:, dc * T + tt * 128: dc * T + tt * 128 + 128],
                                   WV[:, dc * D + eb * 512: dc * D + eb * 512 + 512],
                                   start=(dc == 0), stop=(dc == DC - 1))
            self.c_sc.inc(mm)
            self.c_sc.wait(V)
            ev = V.tensor_copy(V65[:, tt, :, 0:64], ps[:].rearrange("p (h c) -> p h c", c=64))
            self.c_vev.inc(ev)

        if self.stop_at == "vprod":
            return
        def load_w_slices(p, slot):
            ten = self._wrg_tenant[slot]
            if ten is not None:
                P.wait_ge(self.c_qk.h, ten)
            for wi, w_h in ((0, self.wqkv), (1, self.wqkv)):
                src = (self.wqkv[0, :, p * 128:(p + 1) * 128] if wi == 0 else
                       self.wqkv[1, :, p * 128:(p + 1) * 128])
                dst = WRG[:, (2 * slot + wi) * D:(2 * slot + wi) * D + D]
                self.c_wld.dinc(P.dma_start(
                    out=dst.rearrange("p (c m) -> p c m", m=128),
                    in_=src.rearrange("(c p) m -> p c m", p=128)))

        def qk_prod_group(p, slot, wi, tb):
            ring = (2 * p + wi) % 4
            wsl = WRG[:, (2 * slot + wi) * D:(2 * slot + wi) * D + D]
            self.c_qkev.wait(Te)
            for dc in range(DC):
                mm = Te.matmul(self.PS5[:, :],
                               wsl[:, dc * 128:(dc + 1) * 128],
                               XT[:, dc * T + tb * 512: dc * T + tb * 512 + 512],
                               start=(dc == 0), stop=(dc == DC - 1))
            self.c_qk.inc(mm)
            self.c_qk.wait(S)
            ev = S.activation(QKR[:, ring * T + tb * 512: ring * T + tb * 512 + 512],
                              self.PS5[:, :], AF.Copy, scale=(0.125 if wi == 0 else 1.0))
            self.c_qkev.inc(ev)
            return ev

        self.c_vev.wait(Te)
        load_w_slices(0, 0)
        wld_for_pair = {0: self.c_wld.v}
        self._emit_prep(12)
        Te.wait_ge(self.c_wld.h, wld_for_pair[0])
        for wi in range(2):
            for tb in range(4):
                qk_prod_group(0, 0, wi, tb)
        self._wrg_tenant[0] = self.c_qk.v

        for p in range(HP):
            if p + 1 < HP:
                load_w_slices(p + 1, (p + 1) % 2)
                wld_for_pair[p + 1] = self.c_wld.v
                self._wrg_pending = (p + 1) % 2
            rten = self._rot_tenant[p % 2]
            if rten is not None:
                Y.wait_ge(self.c_u.h, rten)
            xl = Y.dma_start(out=ROT[:, (p % 2) * T:(p % 2) * T + T],
                             in_=self.xt[s, p * 128:(p + 1) * 128, :])
            self.c_xld.dinc(xl)
            xld_here = self.c_xld.v
            self._emit_prep(12)

            self._attention_pair(
                p, s, stage_b=False,
                qk_interleave=(lambda qb, _p=p: [
                    qk_prod_group(_p + 1, (_p + 1) % 2, wi_tb // 4, wi_tb % 4)
                    for wi_tb in (2 * qb, 2 * qb + 1)] if _p + 1 < HP else None),
                qk_wait=(lambda eng, _p=p: eng.wait_ge(self.c_wld.h, wld_for_pair[_p + 1]))
                if p + 1 < HP else None,
                zdst=("zst", p))
            if p + 1 < HP:
                self._wrg_tenant[(p + 1) % 2] = self.c_qk.v

            V.wait_ge(self.c_xld.h, xld_here)
            ztile = ZST[:, (p % 2) * T:(p % 2) * T + T]
            ut = V.tensor_tensor(ztile, ztile, ROT[:, (p % 2) * T:(p % 2) * T + T], ALU.add)
            self.c_u.inc(ut)
            self._rot_tenant[p % 2] = self.c_u.v
            self.c_u.wait(Y)
            cu = [self.c_ua, self.c_ub][p % 2]
            cu.dinc(Y.dma_start(out=self.u_scr[p * 128:(p + 1) * 128, :], in_=ztile))
            self._zst_tenant[p % 2] = cu.v

        if self.stop_at == "attn":
            return
        self._ln_epilogue_A(s)

    def _attention_pair(self, p, s, stage_b, qk_interleave, qk_wait, zdst,
                        pre_wait=None):
        nc, Te, V, S, P = self.nc, self.Te, self.V, self.S, self.P
        QKR = self.QKR.ap()
        V65 = self.V65.ap().rearrange("p (k h c) -> p k h c", k=16, h=NH)
        PRbase = 8192 if not stage_b else 0
        PR = self.AR_C.ap() if not stage_b else self.WRG.ap()
        qring = (2 * p) % 4
        kring = (2 * p + 1) % 4
        qof = qring * T
        kof = kring * T

        if zdst[0] == "zst":
            ZST = self.AR_C.ap().bitcast(F32)
            zt = ZST[:, (p % 2) * T:(p % 2) * T + T]
            ten = self._zst_tenant[p % 2]
            if ten is not None:
                [self.c_ua, self.c_ub][p % 2].wait(V, ten)
        else:
            Y2 = self.AR_C.ap()
            zt = None

        if pre_wait is not None:
            pre_wait(Te)
        for qb in range(4):
            if qk_interleave is not None:
                if qk_wait is not None:
                    qk_wait(Te)
                qk_interleave(qb)
            nkt = 16 if not stage_b else (qb + 1) * 4
            y_started = False
            for kt in range(nkt):
                scps = [self.PS0, self.PS1][kt % 2]
                prev_exp = self._sc_tenant[kt % 2]
                if prev_exp is not None:
                    self.c_exp.wait(Te, prev_exp)
                slot = kt % 4
                prev = self._pr_tenant[slot]
                if prev is not None:
                    self.c_av.wait(S, prev)
                m1 = Te.matmul(scps[:, 0:512],
                               QKR[:, kof + kt * 128: kof + kt * 128 + 128][0:64, :],
                               QKR[0:64, qof + qb * 512: qof + qb * 512 + 512],
                               start=True, stop=True, tile_position=(0, 0))
                m2 = Te.matmul(scps[:, 512:1024],
                               QKR[:, kof + kt * 128: kof + kt * 128 + 128][64:128, :],
                               QKR[64:128, qof + qb * 512: qof + qb * 512 + 512],
                               start=True, stop=True, tile_position=(64, 0))
                self.c_sc.inc(m2)
                self.c_sc.wait(S)
                pr_slot = PR[:, PRbase + slot * 1024: PRbase + slot * 1024 + 1024] \
                    if not stage_b else PR[:, slot * 1024: slot * 1024 + 1024]
                ex = S.activation(pr_slot, scps[:, :], AF.Exp)
                self.c_exp.inc(ex)
                self._sc_tenant[kt % 2] = self.c_exp.v
                use_c = self.c_exp
                if stage_b and kt >= nkt - 4:
                    j = kt - (nkt - 4)
                    P.wait_ge(self.c_misc.h, self._msk_ld)
                    self.c_exp.wait(P)
                    mk1 = P.tensor_tensor(pr_slot[:, 0:512], pr_slot[:, 0:512],
                                          self.MSK.ap()[:, j * 512:(j + 1) * 512], ALU.mult)
                    mk2 = P.tensor_tensor(pr_slot[:, 512:1024], pr_slot[:, 512:1024],
                                          self.MSK.ap()[:, j * 512:(j + 1) * 512], ALU.mult)
                    self.c_msk.inc(mk2)
                    use_c = self.c_msk
                if kt == 0:
                    if self._ynorm_pending is not None:
                        self.c_yev.wait(Te, self._ynorm_pending[0])
                        self.c_rcp.wait(Te, self._ynorm_pending[1])
                use_c.wait(Te)
                a1 = Te.matmul(self.PS2[0:65, :], V65[:, kt, 2 * p, :],
                               pr_slot[:, 0:512], start=(kt == 0), stop=(kt == nkt - 1))
                a2 = Te.matmul(self.PS3[0:65, :], V65[:, kt, 2 * p + 1, :],
                               pr_slot[:, 512:1024], start=(kt == 0), stop=(kt == nkt - 1))
                self.c_av.inc(a2)
                self._pr_tenant[slot] = self.c_av.v
            for h in range(2):
                yps = [self.PS2, self.PS3][h]
                self.c_av.wait(V)
                self.c_rb.wait(V)
                with self.nc.allow_low_precision("softmax denominator reciprocal row as f32r for K=1 broadcast matmul"):
                    rc = V.reciprocal(self.RRrow(), yps[64:65, :])
                self.c_rcp.inc(rc)
                row0 = h * 64
                if zdst[0] == "zst":
                    dst = zt[:, qb * 512:(qb + 1) * 512][row0:row0 + 64, :]
                else:
                    Y2 = self.AR_C.ap()
                    dst = Y2[:, p * T + qb * 512: p * T + qb * 512 + 512][row0:row0 + 64, :]
                yc = V.tensor_copy(dst, yps[0:65, :][0:64, :])
                self.c_yev.inc(yc)
                self.c_rcp.wait(Te)
                self.c_zn.wait(Te)
                rb = Te.matmul(self.PS4[0:64, :], self.ONESR.ap()[:, 0:64], self.RRrow(),
                               start=True, stop=True)
                self.c_rb.inc(rb)
                self.c_rb.wait(V)
                zn = V.tensor_tensor(dst, dst, self.PS4[0:64, :], ALU.mult)
                self.c_zn.inc(zn)
            self._ynorm_pending = (self.c_yev.v, self.c_rcp.v)

    def _rows_from_stats(self):
        V, S, Y = self.V, self.S, self.Y
        mu, msq = self.rt_mu(), self.rt_msq()
        var, rstd, r2, tmp = self.rt_var(), self.rt_rstd(), self.rt_r2(), self.rt_tmp()
        self.c_rowd.wait(V)
        self.vsync(V.tensor_tensor(tmp, mu, mu, ALU.mult))
        self.vsync(V.tensor_tensor(var, msq, tmp, ALU.subtract))
        self.vsync(V.tensor_scalar_add(tmp, var, EPS))
        self.rsqrt_rows(rstd, tmp)
        self.vsync(V.scalar_tensor_tensor(r2, mu, -1.0, rstd, ALU.mult, ALU.mult))
        if self.debug:
            self.c_vx.wait(Y)
            self.c_rowd.dinc(Y.dma_start(out=self.dbg_rowt[:, :], in_=self.ROWT.ap()[:, 0:96]))
        self.c_vx.wait(Y)
        self.c_rowd.dinc(Y.dma_start(
            out=self.rows_scr[3:4, :].rearrange("g (p n) -> (g p) n", p=128),
            in_=self.ROWT.ap()[:, 48:64]))
        self.c_rowd.dinc(Y.dma_start(
            out=self.rows_scr[4:5, :].rearrange("g (p n) -> (g p) n", p=128),
            in_=self.ROWT.ap()[:, 64:80]))

    def _stats_pass(self, src_scr, nrm=1.0 / D):
        Te, V, S, P = self.Te, self.V, self.S, self.P
        self.c_sc.wait(P)
        for dc in range(DC):
            if dc > 0:
                self.c_stat.wait(P)
            self.c_uld.dinc(P.dma_start(out=self.URt(),
                                        in_=src_scr[dc * 128:(dc + 1) * 128, :]))
            self.c_uld.wait(S)
            if dc > 0:
                self.c_stat.wait(S)
            sq = S.activation(self.SQRt(), self.URt(), AF.Square)
            self.c_sq.inc(sq)
            self.c_uld.wait(Te)
            self.c_sq.wait(Te)
            mus = [self.PS2, self.PS3, self.PS4, self.PS5]
            for j in range(4):
                Te.matmul(mus[j][0:1, :], self.ONESB.ap()[:, :],
                          self.URt()[:, j * 512:(j + 1) * 512],
                          start=(dc == 0), stop=(dc == DC - 1))
            for j in range(4):
                ps = [self.PS0, self.PS1][j // 2]
                mm = Te.matmul(ps[0:1, (j % 2) * 512:(j % 2) * 512 + 512],
                               self.ONESB.ap()[:, :],
                               self.SQRt()[:, j * 512:(j + 1) * 512],
                               start=(dc == 0), stop=(dc == DC - 1))
            self.c_stat.inc(mm)
        self.c_stat.wait(V)
        Y = self.Y
        mu = self.rs_mu()
        msq = self.rs_msq()
        mus = [self.PS2, self.PS3, self.PS4, self.PS5]
        for j in range(4):
            V.tensor_scalar_mul(mu[:, j * 512:(j + 1) * 512], mus[j][0:1, :], float(nrm))
        for j in range(4):
            ps = [self.PS0, self.PS1][j // 2]
            mm = V.tensor_scalar_mul(msq[:, j * 512:(j + 1) * 512],
                                     ps[0:1, (j % 2) * 512:(j % 2) * 512 + 512], float(nrm))
        self.c_stat.inc(mm)
        self.c_stat.wait(Y)
        self.c_rowd.dinc(Y.dma_start(out=self.rows_scr[0:1, :], in_=mu))
        self.c_rowd.dinc(Y.dma_start(out=self.rows_scr[1:2, :], in_=msq))
        self.c_rowd.wait(Y)
        self.c_rowd.dinc(Y.dma_start(
            out=self.ROWT.ap()[:, 0:16],
            in_=self.rows_scr[0:1, :].rearrange("g (p n) -> (g p) n", p=128)))
        self.c_rowd.dinc(Y.dma_start(
            out=self.ROWT.ap()[:, 16:32],
            in_=self.rows_scr[1:2, :].rearrange("g (p n) -> (g p) n", p=128)))

    def _row_to_f32r(self, dst_r, src_ap, n):
        dd = self.P.dma_start(out=dst_r[:, 0:n], in_=src_ap[:, 0:n])
        self.c_rbc.dinc(dd)

    def _bcast_rows(self, rows_f32r, psums):
        Te = self.Te
        for (row, plist) in zip(rows_f32r, psums):
            for j, (ps, off, rof) in enumerate(plist):
                mm = Te.matmul(ps[:, off:off + 512], self.ONESR.ap()[:, :],
                               row[:, rof:rof + 512], start=True, stop=True)
        self.c_rbc.inc(mm)

    def _ln_epilogue_A(self, s):
        nc, Te, V, S, P, Y = self.nc, self.Te, self.V, self.S, self.P, self.Y
        self.c_ua.wait(P)
        self.c_ub.wait(P)
        for c in (self.c_exp, self.c_yev, self.c_rcp, self.c_zn, self.c_qkev):
            c.wait(Te)
        self._stats_pass(self.u_scr)
        self._rows_from_stats()
        self.c_rowd.wait(P)
        self._row_to_f32r(self.ROWR(), self.rows_scr[3:4, :], T)
        self.c_rbc.wait(Te)
        self.c_stat.wait(Te)
        for j in range(4):
            ps = [self.PS0, self.PS1][j // 2]
            mm = Te.matmul(ps[:, (j % 2) * 512:(j % 2) * 512 + 512], self.ONESR.ap()[:, :],
                           self.ROWR()[:, j * 512:(j + 1) * 512], start=True, stop=True)
        self.c_rbm.inc(mm)
        self.c_rbm.wait(P)
        self._row_to_f32r(self.ROWR(), self.rows_scr[4:5, :], T)
        self.c_rbc.wait(Te)
        for j in range(4):
            ps = [self.PS2, self.PS3, self.PS4, self.PS5][j]
            mm = Te.matmul(ps[:, :], self.ONESR.ap()[:, :],
                           self.ROWR()[:, j * 512:(j + 1) * 512], start=True, stop=True)
        self.c_rbm.inc(mm)
        ROT = self.ROT.ap()
        AIN = self.AR_B.ap()
        self.c_rbm.wait(V)
        self.c_u.wait(Y)
        self.c_rowd.wait(Y)
        for dc in range(DC):
            slot = ROT[:, (dc % 2) * T:(dc % 2) * T + T]
            if dc >= 2:
                [self.c_o0a, self.c_o0b][dc % 2].wait(Y)
            ld = Y.dma_start(out=slot, in_=self.u_scr[dc * 128:(dc + 1) * 128, :])
            self.c_yld.dinc(ld)
            self.c_yld.wait(V)
            for j in range(2):
                V.tensor_tensor(slot[:, j * 1024:(j + 1) * 1024],
                                slot[:, j * 1024:(j + 1) * 1024],
                                [self.PS0, self.PS1][j][:, :], ALU.mult)
            for j in range(4):
                ps = [self.PS2, self.PS3, self.PS4, self.PS5][j]
                o0 = V.tensor_tensor(slot[:, j * 512:(j + 1) * 512],
                                     slot[:, j * 512:(j + 1) * 512], ps[:, :], ALU.add)
            self.c_nrm.inc(o0)
            self.c_nrm.wait(Y)
            co = [self.c_o0a, self.c_o0b][dc % 2]
            co.dinc(Y.dma_start(out=self.out0_scr[dc * 128:(dc + 1) * 128, :], in_=slot))
        var, rstd, tmp = self.rt_var(), self.rt_rstd(), self.rt_tmp()
        t2 = self.rt_mu()
        self.c_nrm.wait(V)
        self.vsync(V.tensor_tensor(tmp, var, rstd, ALU.mult))
        self.vsync(V.tensor_tensor(t2, tmp, rstd, ALU.mult))
        self.vsync(V.tensor_scalar_add(tmp, t2, EPS))
        self.rsqrt_rows(t2, tmp)
        self.c_vx.wait(Y)
        st = Y.dma_start(out=self.rows_scr[5:6, :].rearrange("g (p n) -> (g p) n", p=128),
                         in_=t2)
        self.c_rowd.dinc(st)
        self.c_rowd.wait(P)
        self._row_to_f32r(self.ROWR(), self.rows_scr[5:6, :], T)
        self.c_rbc.dinc(P.dma_start(out=self.WROW(), in_=self.ln1w[:, :]))
        self.c_rbc.wait(Te)
        self.c_nrm.wait(Te)
        self.c_o0a.wait(Y)
        self.c_o0b.wait(Y)
        self.c_pst.wait(V)
        spans_even = [(self.PS0, 0), (self.PS0, 512), (self.PS1, 0), (self.PS1, 512)]
        spans_odd = [(self.PS2, 0), (self.PS3, 0), (self.PS4, 0), (self.PS5, 0)]
        for dc in range(DC):
            slot = ROT[:, (dc % 2) * T:(dc % 2) * T + T]
            if dc >= 2:
                self.c_ain.wait(Y, self._aindone[dc - 2])
            ld = Y.dma_start(out=slot, in_=self.out0_scr[dc * 128:(dc + 1) * 128, :])
            self.c_yld.dinc(ld)
            spans = [spans_even, spans_odd][dc % 2]
            if dc >= 2:
                self.c_ain.wait(Te, self._aindone[dc - 2])
            for j, (ps, off) in enumerate(spans):
                mm = Te.matmul(ps[:, off:off + 512],
                               self.WROW()[:, dc * 128:(dc + 1) * 128],
                               self.ROWR()[:, j * 512:(j + 1) * 512],
                               start=True, stop=True)
            self.c_rbm.inc(mm)
            self.c_yld.wait(V)
            self.c_rbm.wait(V)
            for j, (ps, off) in enumerate(spans):
                ai = V.tensor_tensor(AIN[:, dc * T + j * 512: dc * T + j * 512 + 512],
                                     slot[:, j * 512:(j + 1) * 512],
                                     ps[:, off:off + 512], ALU.mult)
            self.c_ain.inc(ai)
            self._aindone[dc] = self.c_ain.v
            AIN = self.AR_B.ap()
        if self.debug:
            self.c_ain.wait(P)
            for dc in range(DC):
                self.c_cst.dinc(P.dma_start(
                    out=self.dbg_ain[dc * 128:(dc + 1) * 128, :],
                    in_=self.AR_B.ap()[:, dc * T:(dc + 1) * T]))


    def _psum_quiesce(self):
        for c in (self.c_exp, self.c_yev, self.c_rcp, self.c_zn, self.c_qkev,
                  self.c_vev, self.c_stat, self.c_nrm, self.c_ain, self.c_rtr,
                  self.c_silu, self.c_act, self.c_cmb, self.c_wbc, self.c_lat,
                  self.c_rope, self.c_eo, self.c_oproj, self.c_rw):
            c.wait(self.Te)

    def phase_B(self, s):
        nc, Te, V, S, P, Y = self.nc, self.Te, self.V, self.S, self.P, self.Y
        AIN = self.AR_B.ap()
        LATB = self.LATB.ap()
        WRG = self.WRG.ap()
        WRGB = self.WRGB.ap()
        QKR = self.QKR.ap()
        RT = self.AR_A.ap()
        ROT = self.ROT.ap()
        V65 = self.V65.ap().rearrange("p (k h c) -> p k h c", k=16, h=NH)
        if s > 0:
            self.c_misc.dinc(Y.dma_start(
                out=self.MSK.ap()[:, :].rearrange("p (j n) -> p j n", j=4),
                in_=self.masks[:, :, :].rearrange("j p n -> p j n")))
            self._msk_ld = self.c_misc.v
        self._psum_quiesce()

        for wi, w_h in ((0, self.kvdT), (1, self.qdT)):
            self.c_wld.dinc(P.dma_start(
                out=WRG[:, wi * 2048:(wi + 1) * 2048].rearrange("p (c m) -> p c m", m=LAT),
                in_=w_h[:, :].rearrange("(c p) m -> p c m", p=128)))
        wld_b1 = self.c_wld.v
        Te.wait_ge(self.c_wld.h, wld_b1)
        self.c_ain.wait(Te)
        for wi in range(2):
            for li in range(2):
                for tb in range(4):
                    self.c_lat.wait(Te)
                    for dc in range(DC):
                        mm = Te.matmul(self.PS5[:, :],
                                       WRG[:, wi * 2048 + dc * LAT + li * 128:
                                           wi * 2048 + dc * LAT + li * 128 + 128],
                                       AIN[:, dc * T + tb * 512: dc * T + tb * 512 + 512],
                                       start=(dc == 0), stop=(dc == DC - 1))
                    self.c_qk.inc(mm)
                    self.c_qk.wait(S)
                    ev = S.copy(LATB[:, (2 * wi + li) * T + tb * 512:
                                     (2 * wi + li) * T + tb * 512 + 512], self.PS5[:, :])
                    self.c_lat.inc(ev)

        if self.stop_at == "Blat":
            return
        self.c_qk.wait(P)
        self.c_wld.dinc(P.dma_start(
            out=WRG[:, 0:2048].rearrange("p (c m) -> p c m", m=D),
            in_=self.vuT[:, :].rearrange("(c p) m -> p c m", p=128)))
        wld_vu = self.c_wld.v
        self.c_av.wait(V)
        ms = V.memset(self.V65.ap()[:], 1.0)
        self.c_vev.inc(ms)
        Te.wait_ge(self.c_wld.h, wld_vu)
        self.c_lat.wait(Te)
        vev0 = self.c_vev.v
        for tt in range(16):
            ps = [self.PS0, self.PS1][tt % 2]
            if tt >= 2:
                self.c_vev.wait(Te, vev0 + (tt - 2) + 1)
            else:
                self.c_vev.wait(Te, vev0)
            for eb in range(2):
                for lc in range(2):
                    mm = Te.matmul(ps[:, eb * 512:(eb + 1) * 512],
                                   LATB[:, lc * T + tt * 128: lc * T + tt * 128 + 128],
                                   WRG[:, lc * D + eb * 512: lc * D + eb * 512 + 512],
                                   start=(lc == 0), stop=(lc == 1))
            self.c_sc.inc(mm)
            self.c_sc.wait(V)
            ev = V.tensor_copy(V65[:, tt, :, 0:64], ps[:].rearrange("p (h c) -> p h c", c=64))
            self.c_vev.inc(ev)

        self.c_u.wait(Y)
        self.c_nrm.wait(Y)
        self.c_ain.wait(Y)
        self.c_yld.dinc(Y.dma_start(out=ROT[:, 0:T], in_=self.cosqT[:, :]))
        self.c_yld.dinc(Y.dma_start(out=ROT[:, T:2 * T], in_=self.sinxqT[:, :]))
        yld_cos = self.c_yld.v
        V.wait_ge(self.c_yld.h, yld_cos)
        self.c_sc.wait(V)
        self.c_qk.wait(V)

        def rope_pass(kind, wbase, nsrc, rhs_of, pass_i):
            TMP = self.WRGB.ap()[:, 0:512]
            for g in range(4):
                for sub in range(4):
                    self.c_rope.wait(Te)
                    for ci in range(nsrc):
                        mm = Te.matmul(ps2[:, :],
                                       WRG[:, wbase + ci * NH * RD + g * 128:
                                           wbase + ci * NH * RD + g * 128 + 128],
                                       rhs_of(ci, sub),
                                       start=(ci == 0), stop=(ci == nsrc - 1))
                    self.c_rt2.inc(mm)
                    self.c_rt2.wait(V)
                    dst = RT[:, (kind * 4 + g) * T + sub * 512:
                             (kind * 4 + g) * T + sub * 512 + 512]
                    if pass_i == 0:
                        rp = V.tensor_tensor(dst, ps2[:, :],
                                             ROT[:, sub * 512: sub * 512 + 512], ALU.mult)
                    else:
                        V.tensor_tensor(TMP, ps2[:, :],
                                        ROT[:, T + sub * 512: T + sub * 512 + 512], ALU.mult)
                        rp = V.tensor_tensor(dst, dst, TMP, ALU.add)
                    self.c_rope.inc(rp)

        ps2 = self.PS2
        rhs_q = lambda ci, sb: LATB[:, (2 + ci) * T + sb * 512: (2 + ci) * T + sb * 512 + 512]
        rhs_k = lambda ci, sb: AIN[:, ci * T + sb * 512: ci * T + sb * 512 + 512]
        self.c_qk.wait(P)
        for wi, w_h in ((0, self.rqT), (1, self.rqTp)):
            self.c_wld.dinc(P.dma_start(
                out=WRG[:, 2048 + wi * 1024: 2048 + (wi + 1) * 1024]
                    .rearrange("p (c m) -> p c m", m=NH * RD),
                in_=w_h[:, :].rearrange("(c p) m -> p c m", p=128)))
        wld_rq = self.c_wld.v
        Te.wait_ge(self.c_wld.h, wld_rq)
        rope_pass(0, 2048, 2, rhs_q, 0)
        rope_pass(0, 3072, 2, rhs_q, 1)
        self.c_rope.wait(Y)
        self.c_yld.dinc(Y.dma_start(out=ROT[:, 0:T], in_=self.cosT[:, :]))
        self.c_yld.dinc(Y.dma_start(out=ROT[:, T:2 * T], in_=self.sinxT[:, :]))
        V.wait_ge(self.c_yld.h, self.c_yld.v)
        self.c_sc.wait(P)
        self.c_rt2.wait(P)
        for pass_i, w_h in ((0, self.rkT), (1, self.rkTp)):
            self.c_wld.dinc(P.dma_start(
                out=WRG[:, 0:4096].rearrange("p (c m) -> p c m", m=NH * RD),
                in_=w_h[:, :].rearrange("(c p) m -> p c m", p=128)))
            wld_rk = self.c_wld.v
            Te.wait_ge(self.c_wld.h, wld_rk)
            rope_pass(1, 0, DC, rhs_k, pass_i)
            if pass_i == 0:
                self.c_rt2.wait(P)
        del ps2

        def load_kq_slices_B(p, slot):
            self.c_rope.wait(P)
            ten = self._wrg_tenant[slot]
            if ten is not None:
                P.wait_ge(self.c_qk.h, ten)
            for wi, w_h in ((0, self.quT), (1, self.kuT)):
                dst = WRGB[:, (2 * slot + wi) * 256:(2 * slot + wi) * 256 + 256]
                self.c_wld.dinc(P.dma_start(
                    out=dst.rearrange("p (c m) -> p c m", m=128),
                    in_=w_h[:, p * 128:(p + 1) * 128].rearrange("(c p) m -> p c m", p=128)))

        def kq_prod_group_B(p, slot, wi, tb):
            ring = (2 * p + wi) % 4
            wsl = WRGB[:, (2 * slot + wi) * 256:(2 * slot + wi) * 256 + 256]
            lat_base = 2 * T if wi == 0 else 0
            self.c_qkev.wait(Te)
            for lc in range(2):
                mm = Te.matmul(self.PS5[:, :],
                               wsl[:, lc * 128:(lc + 1) * 128],
                               LATB[:, lat_base + lc * T + tb * 512:
                                    lat_base + lc * T + tb * 512 + 512],
                               start=(lc == 0), stop=(lc == 1))
            self.c_qk.inc(mm)
            self.c_qk.wait(S)
            ev = S.activation(QKR[:, ring * T + tb * 512: ring * T + tb * 512 + 512],
                              self.PS5[:, :], AF.Copy, scale=(0.125 if wi == 0 else 1.0))
            self.c_qkev.inc(ev)

        def rope_scatter(p):
            g = p // 2
            j0 = 2 * (p % 2)
            self.c_qkev.wait(P)
            self.c_rope.wait(P)
            for wi in range(2):
                ring = (2 * p + wi) % 4
                src_t = (wi * 4 + g) * T
                for j, dst0 in ((j0, 0), (j0 + 1, 64)):
                    cp = P.tensor_copy(
                        QKR[:, ring * T: ring * T + T][dst0:dst0 + RD, :],
                        RT[:, src_t: src_t + T][j * RD:(j + 1) * RD, :])
            self.c_scat.inc(cp)

        if self.stop_at == "Brope":
            return
        self.c_vev.wait(Te)
        load_kq_slices_B(0, 0)
        wld_for_pair = {0: self.c_wld.v}
        Te.wait_ge(self.c_wld.h, wld_for_pair[0])
        for wi in range(2):
            for tb in range(4):
                kq_prod_group_B(0, 0, wi, tb)
        self._wrg_tenant[0] = self.c_qk.v
        rope_scatter(0)
        scat_for_pair = {0: self.c_scat.v}
        if self.stop_at == "Bprod0":
            return

        for p in range(HP):
            if p + 1 < HP:
                load_kq_slices_B(p + 1, (p + 1) % 2)
                wld_for_pair[p + 1] = self.c_wld.v

            def interleave(qb, _p=p):
                if _p + 1 >= HP:
                    return
                for wi_tb in (2 * qb, 2 * qb + 1):
                    kq_prod_group_B(_p + 1, (_p + 1) % 2, wi_tb // 4, wi_tb % 4)
                if qb == 3:
                    rope_scatter(_p + 1)
                    scat_for_pair[_p + 1] = self.c_scat.v

            self._attention_pair(
                p, s, stage_b=True,
                qk_interleave=interleave if p + 1 < HP else None,
                qk_wait=(lambda eng, _p=p: eng.wait_ge(self.c_wld.h, wld_for_pair[_p + 1]))
                if p + 1 < HP else None,
                zdst=("y2", p),
                pre_wait=lambda eng, _p=p: eng.wait_ge(self.c_scat.h, scat_for_pair[_p]))
            if p + 1 < HP:
                self._wrg_tenant[(p + 1) % 2] = self.c_qk.v
            if self.stop_at == "Batt0" and p == int(_os.environ.get("MK_BP", "0")):
                self.c_zn.wait(P)
                self.c_cst.dinc(P.dma_start(out=self.dbg_y2[0:128, :],
                                            in_=self.AR_C.ap()[:, 0:T]))
                return

        if self.debug:
            self.c_zn.wait(P)
            for ec in range(DC):
                self.c_cst.dinc(P.dma_start(
                    out=self.dbg_y2[ec * 128:(ec + 1) * 128, :],
                    in_=self.AR_C.ap()[:, ec * T:(ec + 1) * T]))

        if self.stop_at == "B5dmp":
            return
        self._psum_quiesce()
        self.c_scat.wait(P)
        self.c_sc.wait(P)
        for ec in range(DC):
            self.c_uld.dinc(P.dma_start(
                out=self.AR_A.ap()[:, ec * D:(ec + 1) * D],
                in_=self.owT[ec * 128:(ec + 1) * 128, :]))
        uld_ow = self.c_uld.v
        if self.stop_at == "B5ow":
            return
        Te.wait_ge(self.c_uld.h, uld_ow)
        Y2 = self.AR_C.ap()
        OWB = self.AR_A.ap()
        self.c_rope.wait(Y)
        for dc in range(DC):
            cpar = [self.c_o0a, self.c_o0b][dc % 2]
            cpar.wait(Y)
            self.c_nrm.wait(Y)
            slot = ROT[:, (dc % 2) * T:(dc % 2) * T + T]
            ld = Y.dma_start(out=slot, in_=self.out0_scr[dc * 128:(dc + 1) * 128, :])
            self.c_yld.dinc(ld)
            yld_here = self.c_yld.v
            for nb in range(4):
                ps = [self.PS0, self.PS1][nb % 2]
                pcol = (nb % 2) * 0
                if nb >= 2:
                    self.c_nrm.wait(Te)
                for ec in range(DC):
                    mm = Te.matmul(ps[:, 0:512],
                                   OWB[:, ec * D + dc * 128: ec * D + dc * 128 + 128],
                                   Y2[:, ec * T + nb * 512: ec * T + nb * 512 + 512],
                                   start=(ec == 0), stop=(ec == DC - 1))
                self.c_oproj.inc(mm)
                self.c_oproj.wait(V)
                V.wait_ge(self.c_yld.h, yld_here)
                h1t = V.tensor_tensor(slot[:, nb * 512: nb * 512 + 512],
                                      slot[:, nb * 512: nb * 512 + 512],
                                      ps[:, 0:512], ALU.add)
                self.c_nrm.inc(h1t)
            self.c_nrm.wait(Y)
            cpar.dinc(Y.dma_start(out=self.h1_scr[dc * 128:(dc + 1) * 128, :], in_=slot))

        if self.stop_at == "B5oproj":
            return
        self.c_o0a.wait(P)
        self.c_o0b.wait(P)
        self._stats_pass(self.h1_scr)
        self._rows_from_stats()
        self.c_rowd.wait(P)
        self._row_to_f32r(self.ROWR(), self.rows_scr[3:4, :], T)
        self.c_rbc.wait(Te)
        self.c_stat.wait(Te)
        self.c_ain.wait(Te)
        for j in range(4):
            ps = [self.PS0, self.PS1][j // 2]
            mm = Te.matmul(ps[:, (j % 2) * 512:(j % 2) * 512 + 512], self.ONESR.ap()[:, :],
                           self.ROWR()[:, j * 512:(j + 1) * 512], start=True, stop=True)
        self.c_rbm.inc(mm)
        self.c_rbm.wait(P)
        self._row_to_f32r(self.ROWR(), self.rows_scr[4:5, :], T)
        self.c_rbc.wait(Te)
        for j in range(4):
            ps = [self.PS2, self.PS3, self.PS4, self.PS5][j]
            mm = Te.matmul(ps[:, :], self.ONESR.ap()[:, :],
                           self.ROWR()[:, j * 512:(j + 1) * 512], start=True, stop=True)
        self.c_rbm.inc(mm)
        if self.stop_at == "B5stats":
            return
        self.c_rowd.wait(Y)
        self.c_rbm.wait(V)
        self.c_o0a.wait(Y)
        self.c_o0b.wait(Y)
        MIN = self.AR_B.ap()
        for dc in range(DC):
            slot = ROT[:, (dc % 2) * T:(dc % 2) * T + T]
            if dc >= 2:
                self.c_ain.wait(Y, self._aindone[dc - 2])
            ld = Y.dma_start(out=slot, in_=self.h1_scr[dc * 128:(dc + 1) * 128, :])
            self.c_yld.dinc(ld)
            self.c_yld.wait(V)
            for j in range(2):
                V.tensor_tensor(slot[:, j * 1024:(j + 1) * 1024],
                                slot[:, j * 1024:(j + 1) * 1024],
                                [self.PS0, self.PS1][j][:, :], ALU.mult)
            for j in range(4):
                ps = [self.PS2, self.PS3, self.PS4, self.PS5][j]
                o0 = V.tensor_tensor(slot[:, j * 512:(j + 1) * 512],
                                     slot[:, j * 512:(j + 1) * 512], ps[:, :], ALU.add)
            self.vsync(o0)
            ai = V.tensor_scalar_mul(MIN[:, dc * T:(dc + 1) * T], slot,
                                     self.WCOL.ap()[:, 8 + dc:8 + dc + 1])
            self.c_ain.inc(ai)
            self._aindone[dc] = self.c_ain.v
        if self.debug:
            self.c_ain.wait(P)
            for dc in range(DC):
                self.c_cst.dinc(P.dma_start(
                    out=self.dbg_min[dc * 128:(dc + 1) * 128, :],
                    in_=MIN[:, dc * T:(dc + 1) * T]))

    def phase_C(self, s):
        nc, Te, V, S, P, Y = self.nc, self.Te, self.V, self.S, self.P, self.Y
        MIN = self.AR_B.ap()
        RTR = self.RTR.ap()
        lg = RTR[:, 0:8]
        m8 = RTR[:, 8:16]
        zz = RTR[:, 16:24]
        ind = RTR[:, 24:32]
        probs = RTR[:, 32:40]
        pw = RTR[:, 40:48]
        den = RTR[:, 48:49]
        rden = RTR[:, 49:50]
        wgt = RTR[:, 56:64]
        stg = RTR[:, 96:224]
        self._psum_quiesce()
        self.c_cst.wait(Te)
        for tt in range(16):
            ps = [self.PS2, self.PS3][tt % 2]
            if tt >= 2:
                Te.wait_ge(self.c_vx.h, self._rtr_t[tt % 2])
            for dc in range(DC):
                Te.matmul(ps[:, 0:8],
                          MIN[:, dc * T + tt * 128: dc * T + tt * 128 + 128],
                          self.RW8.ap()[:, dc * 8:(dc + 1) * 8],
                          start=(dc == 0), stop=False)
            mm = Te.matmul(ps[:, 0:8], self.ONESR.ap()[:, :], self.BIAS8r(),
                           start=False, stop=True)
            self.c_rtr.inc(mm)
            self.c_rtr.wait(V)
            self.vsync(V.tensor_copy(lg, ps[:, 0:8]))
            self._rtr_t[tt % 2] = self.c_vx.v
            self.vsync(V.memset(RTR[:, 7:8], NEG))
            self.c_vx.wait(S)
            self.ssync(S.activation(probs, lg, AF.Sigmoid))
            self.vsync(V.max(out=m8, in_=lg))
            self.vsync(V.memset(RTR[:, 10:16], NEG))
            self.vsync(V.match_replace(out=zz, in_to_replace=m8, in_values=lg,
                                       imm_value=NEG))
            self.vsync(V.tensor_sub(ind, lg, zz))
            self.vsync(V.tensor_scalar_min(ind, ind, 1.0))
            self.c_sx.wait(V)
            self.vsync(V.tensor_mul(pw, probs, ind))
            self.vsync(V.reduce_sum(den, pw, axis=AX.X))
            self.vsync(V.reciprocal(rden, den))
            self.vsync(V.tensor_scalar_mul(wgt, pw, rden))
            self.c_vx.wait(Te)
            self.c_rw.wait(Te)
            tr = Te.transpose(self.PS4[0:8, 0:128], wgt, self.IDF.ap()[:, :])
            self.c_eo.inc(tr)
            self.c_eo.wait(V)
            cpw = V.tensor_copy(stg[0:8, :], self.PS4[0:8, 0:128])
            self.c_rw.inc(cpw)
            self.c_rw.wait(Y)
            self.c_misc.dinc(Y.dma_start(out=self.wgt_scr[:, tt * 128:(tt + 1) * 128],
                                         in_=stg[0:8, :]))

    def phase_D(self, s):
        nc, Te, V, S, P, Y = self.nc, self.Te, self.V, self.S, self.P, self.Y
        MIN = self.AR_B.ap()
        ACT = self.AR_A.ap()
        SH = self.AR_C.ap().bitcast(F32)
        ROT = self.ROT.ap()
        WGB = self.MSK.ap().bitcast(F32)
        GU = self.V65.ap()
        self._psum_quiesce()
        self.c_av.wait(Y)
        self.c_zn.wait(Y)
        self.c_oproj.wait(Y)
        self.c_ain.wait(Y)
        for half in range(2):
            col0 = half * 1024
            for hc in range(DC):
                self.c_cmb.wait(Y)
                ld1 = Y.dma_start(out=SH[:, hc * 1024:(hc + 1) * 1024],
                                  in_=self.h1_scr[hc * 128:(hc + 1) * 128, col0:col0 + 1024])
                self.c_sld.dinc(ld1)
                ld2 = Y.dma_start(out=ROT[:, (hc % 2) * T:(hc % 2) * T + 1024],
                                  in_=self.out0_scr[hc * 128:(hc + 1) * 128, col0:col0 + 1024])
                self.c_sld.dinc(ld2)
                self.c_sld.wait(V)
                ad = V.tensor_tensor(SH[:, hc * 1024:(hc + 1) * 1024],
                                     SH[:, hc * 1024:(hc + 1) * 1024],
                                     ROT[:, (hc % 2) * T:(hc % 2) * T + 1024], ALU.add)
                self.c_cmb.inc(ad)
            for e in range(NE):
                if e > 0:
                    self.c_rbm.wait(P)
                    self.c_misc.wait(P)
                    self._row_to_f32r(self.ROWR(), self.wgt_scr[e - 1:e, col0:col0 + 1024], 1024)
                    self.c_rbc.wait(Te)
                    self.c_wbc.wait(Te)
                    for j in range(2):
                        ps = [self.PS2, self.PS3][j]
                        mm = Te.matmul(ps[:, :], self.ONESR.ap()[:, :],
                                       self.ROWR()[:, j * 512:(j + 1) * 512],
                                       start=True, stop=True)
                    self.c_rbm.inc(mm)
                    self.c_rbm.wait(V)
                    V.tensor_copy(WGB[:, 0:512], self.PS2[:, :])
                    wb = V.tensor_copy(WGB[:, 512:1024], self.PS3[:, :])
                    self.c_wbc.inc(wb)
                def gu_load(it):
                    slot = it % 4
                    ten = self._gu_tenant[slot]
                    if ten is not None:
                        Y.wait_ge(self.c_gmm.h, ten)
                    gl = Y.dma_start(
                        out=GU[:, slot * 2048: slot * 2048 + 2048]
                            .rearrange("p (c m) -> p c m", m=256),
                        in_=self.gu_bf[e, :, it * 256:(it + 1) * 256]
                            .rearrange("(c p) m -> p c m", p=128))
                    self.c_gld.dinc(gl)
                    return self.c_gld.v

                gld_for = {0: gu_load(0)}
                for it in range(IT):
                    slot = it % 4
                    if it + 1 < IT:
                        gld_for[it + 1] = gu_load(it + 1)
                    Te.wait_ge(self.c_gld.h, gld_for[it])
                    if it >= 1:
                        self.c_silu.wait(Te)
                    for nb in range(2):
                        for dc in range(DC):
                            mm = Te.matmul(
                                self.PS0[:, nb * 512: nb * 512 + 512],
                                GU[:, slot * 2048 + dc * 256:
                                    slot * 2048 + dc * 256 + 128],
                                MIN[:, dc * T + col0 + nb * 512:
                                    dc * T + col0 + nb * 512 + 512],
                                start=(dc == 0), stop=(dc == DC - 1))
                    self.c_gat.inc(mm)
                    if it >= 1:
                        self.c_act.wait(Te)
                    for nb in range(2):
                        for dc in range(DC):
                            mm = Te.matmul(
                                self.PS1[:, nb * 512: nb * 512 + 512],
                                GU[:, slot * 2048 + dc * 256 + 128:
                                    slot * 2048 + dc * 256 + 256],
                                MIN[:, dc * T + col0 + nb * 512:
                                    dc * T + col0 + nb * 512 + 512],
                                start=(dc == 0), stop=(dc == DC - 1))
                    self.c_gmm.inc(mm)
                    self._gu_tenant[slot] = self.c_gmm.v
                    self.c_gat.wait(S)
                    self.c_act.wait(S)
                    sg = S.activation(ROT[:, (it % 2) * T:(it % 2) * T + 1024],
                                      self.PS0[:, :], AF.Silu)
                    self.c_silu.inc(sg)
                    self.c_silu.wait(V)
                    self.c_gmm.wait(V)
                    ac = V.tensor_tensor(ACT[:, it * 1024:(it + 1) * 1024],
                                         ROT[:, (it % 2) * T:(it % 2) * T + 1024],
                                         self.PS1[:, :], ALU.mult)
                    self.c_act.inc(ac)
                def dn_load(ht):
                    slot = ht % 4
                    ten = self._dn_tenant[slot]
                    if ten is not None:
                        Y.wait_ge(self.c_eo.h, ten)
                    dl = Y.dma_start(
                        out=GU[:, 8192 + slot * 2048: 8192 + slot * 2048 + 2048]
                            .rearrange("p (c m) -> p c m", m=128),
                        in_=self.dn_bf[e, :, ht * 128:(ht + 1) * 128]
                            .rearrange("(c p) m -> p c m", p=128))
                    self.c_dld.dinc(dl)
                    return self.c_dld.v

                dld_for = {0: dn_load(0)}
                for ht in range(DC):
                    slot = ht % 4
                    if ht + 1 < DC:
                        dld_for[ht + 1] = dn_load(ht + 1)
                    Te.wait_ge(self.c_dld.h, dld_for[ht])
                    self.c_act.wait(Te)
                    for nb in range(2):
                        ps = [self.PS4, self.PS5][nb]
                        if ht >= 1:
                            self.c_cmb.wait(Te, self._eo_t[nb])
                        for ic in range(IT):
                            mm = Te.matmul(ps[:, :],
                                           GU[:, 8192 + slot * 2048 + ic * 128:
                                               8192 + slot * 2048 + ic * 128 + 128],
                                           ACT[:, ic * 1024 + nb * 512:
                                               ic * 1024 + nb * 512 + 512],
                                           start=(ic == 0), stop=(ic == IT - 1))
                        self.c_eo.inc(mm)
                        self._eo_t[nb] = None
                        self.c_eo.wait(V)
                        hc_col = ht * 1024 + nb * 512
                        if e == 0:
                            cb = V.tensor_tensor(SH[:, hc_col:hc_col + 512],
                                                 SH[:, hc_col:hc_col + 512],
                                                 ps[:, :], ALU.add)
                            self.c_cmb.inc(cb)
                        else:
                            self.c_wbc.wait(V)
                            V.tensor_tensor(ROT[:, 1024 + nb * 512: 1024 + nb * 512 + 512],
                                            ps[:, :], WGB[:, nb * 512:(nb + 1) * 512], ALU.mult)
                            cb = V.tensor_tensor(SH[:, hc_col:hc_col + 512],
                                                 SH[:, hc_col:hc_col + 512],
                                                 ROT[:, 1024 + nb * 512: 1024 + nb * 512 + 512],
                                                 ALU.add)
                            self.c_cmb.inc(cb)
                        self._eo_t[nb] = self.c_cmb.v
                    self._dn_tenant[slot] = self.c_eo.v
            self.c_cmb.wait(Y)
            for hc in range(DC):
                st = Y.dma_start(out=self.s_scr[hc * 128:(hc + 1) * 128, col0:col0 + 1024],
                                 in_=SH[:, hc * 1024:(hc + 1) * 1024])
                self.c_sdma.dinc(st)
            self.c_sdma.wait(Y)

    def phase_E(self, s):
        nc, Te, V, S, P, Y = self.nc, self.Te, self.V, self.S, self.P, self.Y
        ROT = self.ROT.ap()
        self._psum_quiesce()
        self.c_sdma.wait(P)
        self.c_silu.wait(Y)
        self.c_cmb.wait(Y)
        self._stats_pass(self.s_scr)
        self._rows_from_stats()
        self.c_rowd.wait(P)
        self._row_to_f32r(self.ROWR(), self.rows_scr[3:4, :], T)
        self.c_rbc.wait(Te)
        self.c_stat.wait(Te)
        for j in range(4):
            ps = [self.PS0, self.PS1][j // 2]
            mm = Te.matmul(ps[:, (j % 2) * 512:(j % 2) * 512 + 512], self.ONESR.ap()[:, :],
                           self.ROWR()[:, j * 512:(j + 1) * 512], start=True, stop=True)
        self.c_rbm.inc(mm)
        self.c_rbm.wait(P)
        self._row_to_f32r(self.ROWR(), self.rows_scr[4:5, :], T)
        self.c_rbc.wait(Te)
        for j in range(4):
            ps = [self.PS2, self.PS3, self.PS4, self.PS5][j]
            mm = Te.matmul(ps[:, :], self.ONESR.ap()[:, :],
                           self.ROWR()[:, j * 512:(j + 1) * 512], start=True, stop=True)
        self.c_rbm.inc(mm)
        self.c_rbm.wait(V)
        self.c_rowd.wait(Y)
        for dc in range(DC):
            cpar = [self.c_o0a, self.c_o0b][dc % 2]
            slot = ROT[:, (dc % 2) * T:(dc % 2) * T + T]
            cpar.wait(Y)
            self.c_nrm.wait(Y)
            ld = Y.dma_start(out=slot, in_=self.s_scr[dc * 128:(dc + 1) * 128, :])
            self.c_yld.dinc(ld)
            self.c_yld.wait(V)
            for j in range(2):
                V.tensor_tensor(slot[:, j * 1024:(j + 1) * 1024],
                                slot[:, j * 1024:(j + 1) * 1024],
                                [self.PS0, self.PS1][j][:, :], ALU.mult)
            for j in range(4):
                ps = [self.PS2, self.PS3, self.PS4, self.PS5][j]
                o0 = V.tensor_tensor(slot[:, j * 512:(j + 1) * 512],
                                     slot[:, j * 512:(j + 1) * 512], ps[:, :], ALU.add)
            self.c_nrm.inc(o0)
            self.c_nrm.wait(Y)
            cpar.dinc(Y.dma_start(out=self.out_t[s, dc * 128:(dc + 1) * 128, :], in_=slot))

    def _reset_seq_state(self):
        self._sc_tenant = [None, None]
        self._pr_tenant = [None, None, None, None]
        self._ynorm_pending = None
        self._zst_tenant = [None, None]
        self._rot_tenant = [None, None]
        self._wrg_tenant = [None, None]
        self._o0dma = {}
        self._aindone = {}
        self._rtr_t = [None, None]
        self._gu_tenant = [None, None, None, None]
        self._dn_tenant = [None, None, None, None]
        self._eo_t = [None, None]

    def _seq_barrier(self):
        for eng in (self.Te, self.V, self.S, self.P, self.Y):
            for c in vars(self).values():
                if isinstance(c, Ctr) and c.v > 0:
                    c.wait(eng)

    def build(self):
        self.prep_and_consts()
        if self.stop_at == "prep":
            self.finish()
            return
        for s in range(self.n_seq):
            if s > 0:
                self._seq_barrier()
            self._reset_seq_state()
            self.phase_A(s)
            if self.last_phase == "A" or self.stop_at in ("load", "vprod", "attn", "xt"):
                break
            self.phase_B(s)
            if self.last_phase == "B" or self.stop_at is not None:
                break
            self.phase_C(s)
            if self.last_phase == "C":
                break
            self.phase_D(s)
            if self.last_phase == "D":
                break
            self.phase_E(s)
        self.finish()

    def finish(self):
        for c in vars(self).values():
            if isinstance(c, Ctr) and c.v > 0:
                c.wait(self.Y)


def kernel(**inputs):
    in_maps = build_host_inputs(inputs)
    nc = bass.Bass("TRN2", target_bir_lowering=False)
    mk = MK(nc)
    mk.build()
    res = run_bass_kernel_spmd(nc, in_maps, core_ids=list(range(NCORE)))
    outs = []
    for c in range(NCORE):
        o = res.results[c]["out_t"]
        outs.append(np.ascontiguousarray(o.transpose(0, 2, 1)))
    return np.concatenate(outs, 0).astype(np.float32)
